# version 01 (final)
# baseline (speedup 1.0000x reference)
import numpy as np
import contextlib
import concourse.bass as bass
import concourse.mybir as mybir
from concourse.bass_utils import run_bass_kernel_spmd
import ml_dtypes

F32 = mybir.dt.float32
BF16 = mybir.dt.bfloat16
U8 = mybir.dt.uint8
AF = mybir.ActivationFunctionType
OP = mybir.AluOpType
AX = mybir.AxisListType
NBF = ml_dtypes.bfloat16

B, N, D = 4, 4096, 1024
TOWN = 2048
SLAB = 512
SHIFT = 30.0

def np2dt(a):
    d = a.dtype
    if d == np.float32: return F32
    if d == np.uint8 or d == np.bool_: return U8
    if d == NBF: return BF16
    raise ValueError(f"bad dtype {d}")

class Prog:
    ENGINES = ("sync", "tensor", "vector", "scalar", "gpsimd")
    def __init__(self):
        self.nc = bass.Bass("TRN2", target_bir_lowering=False)
        self.st = contextlib.ExitStack()
        self.chain = self.st.enter_context(self.nc.semaphore("chain"))
        self.cnt = 0
        self.seq = {e: [] for e in self.ENGINES}
        self._n = 0
        self.consts = {}
        self._fin = False
        self._ctiles = {}
    def _name(self, p):
        self._n += 1
        return f"{p}{self._n}"
    def inp(self, name, shape, dtype):
        return self.nc.dram_tensor(name, list(shape), dtype, kind="ExternalInput").ap()
    def outp(self, name, shape, dtype):
        return self.nc.dram_tensor(name, list(shape), dtype, kind="ExternalOutput").ap()
    def const(self, arr, name=None):
        name = name or self._name("cst")
        arr = np.ascontiguousarray(arr)
        self.consts[name] = arr
        return self.nc.dram_tensor(name, list(arr.shape), np2dt(arr), kind="ExternalInput").ap()
    def sb(self, shape, dtype=F32):
        return self.st.enter_context(self.nc.sbuf_tensor(self._name("s"), list(shape), dtype))
    def ps(self, shape, dtype=F32):
        return self.st.enter_context(self.nc.psum_tensor(self._name("p"), list(shape), dtype))
    def ctile(self, value, parts=128):
        key = (float(value), parts)
        if key not in self._ctiles:
            t = self.sb([parts, 1], F32)
            self.g("vector", lambda e, t=t, v=float(value): e.memset(t[:, :], v))
            self._ctiles[key] = t
        return self._ctiles[key][:, 0:1]
    def act(self, out, in_, func, bias=0.0, scale=1.0, engine="scalar"):
        np_ = out.shape[0]
        b = self.ctile(bias)[0:np_, 0:1] if isinstance(bias, float) else bias
        s = self.ctile(scale)[0:np_, 0:1] if isinstance(scale, float) else scale
        self.g(engine, lambda e: e.activation(out, in_, func, bias=b, scale=s))
    def g(self, engine, *fns):
        assert fns
        self.seq[engine].append((self.cnt, list(fns), len(fns), False))
        self.cnt += len(fns)
    def dma(self, *pairs, engine="sync", **kw):
        fns = [(lambda e, o=o, i=i: e.dma_start(out=o, in_=i, **kw)) for (o, i) in pairs]
        self.seq[engine].append((self.cnt, fns, 16 * len(fns), True))
        self.cnt += 16 * len(fns)
    def mm(self, out_ps, lhsTs, rhss, accum=False):
        n = len(lhsTs)
        fns = []
        for i, (l, r) in enumerate(zip(lhsTs, rhss)):
            st = (i == 0) and (not accum)
            sp = (i == n - 1)
            fns.append(lambda e, l=l, r=r, st=st, sp=sp: e.matmul(out_ps, l, r, start=st, stop=sp))
        self.g("tensor", *fns)
    def finalize(self):
        if self._fin: return
        self._fin = True
        total = self.cnt
        with self.nc.Block() as block:
            def mk(ename):
                def body(eng):
                    for (before, fns, inc, isdma) in self.seq[ename]:
                        if before > 0:
                            eng.wait_ge(self.chain, before)
                        if isdma:
                            for fn in fns:
                                fn(eng).then_inc(self.chain, 16)
                        else:
                            insts = [fn(eng) for fn in fns]
                            insts[-1].then_inc(self.chain, inc)
                    eng.wait_ge(self.chain, total)
                return body
            for ename in self.ENGINES:
                if self.seq[ename] or ename == "sync":
                    getattr(block, ename)(mk(ename))
        self.st.close()

def runp(p, percore, **kw):
    p.finalize()
    in_maps = [dict(p.consts, **pc) for pc in percore]
    res = run_bass_kernel_spmd(p.nc, in_maps, core_ids=list(range(len(percore))), **kw)
    return res

class Ctx:
    def __init__(self, p, nps=4, scr_w=4352):
        self.p = p
        self.ones_col_f = p.sb([128, 1], F32)
        self.ones_row_f = p.sb([1, 128], F32)
        self.ones_col_b = p.sb([128, 1], BF16)
        p.g("vector",
            lambda e: e.memset(self.ones_col_f[:, :], 1.0),
            lambda e: e.memset(self.ones_row_f[:, :], 1.0),
            lambda e: e.memset(self.ones_col_b[:, :], 1.0))
        self.scr = p.sb([128, scr_w])
        self.scr2 = p.sb([128, scr_w])
        self.rows = p.sb([1, 1536])
        self.rows2 = p.sb([1, 1024])
        self.ps = [p.ps([128, 512]) for _ in range(nps)]
        self.ps1 = p.ps([1, 512])
    def scrt(self, i, S=SLAB):
        return self.scr[:, i * S:(i + 1) * S]
    def scr2t(self, i, S=SLAB):
        return self.scr2[:, i * S:(i + 1) * S]

def ln_cm(ctx, xts, S, outs, eps=1e-5, gpp=None, bpp=None, rms=False, wpp=None, sqts=None,
          var_mul=1.0, second=None):
    p = ctx.p
    if sqts is None:
        sqts = [ctx.scr2t(i, S) for i in range(8)]
    ssum = ctx.ps1[0:1, 0:S]
    if not rms:
        p.mm(ssum, [ctx.ones_col_f[:, 0:1]] * 8, xts)
        mean = ctx.rows[0:1, 0:S]
        p.g("vector", lambda e: e.tensor_scalar(mean, ssum, 1.0 / D, None, OP.mult))
    b0 = p.ctile(0.0); s1c = p.ctile(1.0)
    p.g("scalar", *[(lambda e, a=xts[i], o=sqts[i]: e.activation(o, a, AF.Square, bias=b0, scale=s1c)) for i in range(8)])
    p.mm(ssum, [ctx.ones_col_f[:, 0:1]] * 8, [sqts[i] for i in range(8)])
    var = ctx.rows2[0:1, 0:S]
    if rms:
        p.g("vector", lambda e: e.tensor_scalar(var, ssum, 1.0 / D, None, OP.mult))
    else:
        m2 = ctx.rows[0:1, 1024:1024 + S]
        ctx.p.act(m2, mean, AF.Square)
        p.g("vector", lambda e: e.scalar_tensor_tensor(var, ssum, 1.0 / D, m2, OP.mult, OP.subtract))
    std = ctx.rows2[0:1, 512:512 + S]
    ctx.p.act(std, var, AF.Sqrt, bias=eps, scale=var_mul)
    rstd = ctx.rows[0:1, 512:512 + S]
    p.g("vector", lambda e: e.reciprocal(rstd, std))
    bps = ctx.ps[0][:, 0:S]
    p.mm(bps, [ctx.ones_row_f[0:1, :]], [rstd])
    if not rms:
        bpm = ctx.ps[1][:, 0:S]
        p.mm(bpm, [ctx.ones_row_f[0:1, :]], [mean])
    for i in range(8):
        if rms:
            if wpp is not None:
                p.g("vector", lambda e, i=i: e.tensor_mul(sqts[i], xts[i], bps))
                p.g("vector", lambda e, i=i: e.tensor_scalar(outs[i], sqts[i], wpp[i], None, OP.mult))
            else:
                p.g("vector", lambda e, i=i: e.tensor_mul(outs[i], xts[i], bps))
        else:
            p.g("vector", lambda e, i=i: e.tensor_sub(sqts[i], xts[i], bpm))
            if gpp is not None:
                p.g("vector", lambda e, i=i: e.tensor_mul(sqts[i], sqts[i], bps))
                p.g("vector", lambda e, i=i: e.tensor_scalar(outs[i], sqts[i], gpp[i], bpp[i], OP.mult, OP.add))
            else:
                p.g("vector", lambda e, i=i: e.tensor_mul(outs[i], sqts[i], bps))
    if second is not None:
        outs2, vm2, va2, wpp2 = second
        assert not rms and gpp is None
        std2 = ctx.rows2[0:1, 512:512 + S]
        ctx.p.act(std2, var, AF.Sqrt, bias=va2, scale=vm2)
        rstd2 = ctx.rows[0:1, 512:512 + S]
        p.g("vector", lambda e: e.reciprocal(rstd2, std2))
        p.mm(bps, [ctx.ones_row_f[0:1, :]], [rstd2])
        for i in range(8):
            p.g("vector", lambda e, i=i: e.tensor_mul(sqts[i], sqts[i], bps))
            if wpp2 is not None:
                p.g("vector", lambda e, i=i: e.tensor_scalar(outs2[i], sqts[i], wpp2[i], None, OP.mult))
            else:
                p.g("vector", lambda e, i=i: e.tensor_copy(outs2[i], sqts[i]))

def load_w_bf16(ctx, hbm, K, NN, dst, dstoff=0):
    p = ctx.p
    nk = (K + 127) // 128
    per = max(1, 4352 // NN)
    k = 0
    while k < nk:
        kk = min(per, nk - k)
        prs = []
        for j in range(kk):
            rows = min(128, K - (k + j) * 128)
            prs.append((ctx.scr[0:rows, j * NN:j * NN + NN], hbm[(k + j) * 128:(k + j) * 128 + rows, :]))
        p.dma(*prs)
        fns = []
        for j in range(kk):
            rows = min(128, K - (k + j) * 128)
            fns.append(lambda e, j=j, rows=rows, k=k: e.tensor_copy(
                dst[0:rows, dstoff + (k + j) * NN: dstoff + (k + j) * NN + NN],
                ctx.scr[0:rows, j * NN:j * NN + NN]))
        p.g("vector", *fns)
        k += kk

def wview(dst, k, NN, dstoff=0, rows=128):
    return dst[0:rows, dstoff + k * NN: dstoff + k * NN + NN]

_progcache = {}

def get_prog(name):
    if name in _progcache:
        return _progcache[name]
    p = _PROG_BUILDERS[name]()
    _progcache[name] = p
    return p

def build_ln_only():
    p = Prog()
    x = p.inp("x", [D, TOWN], F32)
    y = p.outp("y", [D, TOWN], F32)
    ctx = Ctx(p)
    for s in range(4):
        xts = [ctx.scrt(i) for i in range(8)]
        p.dma(*[(xts[i], x[i * 128:(i + 1) * 128, s * SLAB:(s + 1) * SLAB]) for i in range(8)])
        outs = xts
        ln_cm(ctx, xts, SLAB, outs)
        p.dma(*[(y[i * 128:(i + 1) * 128, s * SLAB:(s + 1) * SLAB], outs[i]) for i in range(8)])
    return p

def build_l0a():
    p = Prog()
    x = p.inp("x", [D, TOWN], F32)
    wv = p.inp("wv", [D, 256], F32)
    wo = p.inp("wo", [D, D], F32)
    y = p.outp("y", [D, TOWN], F32)
    ctx = Ctx(p)
    wvb = p.sb([128, 8 * 256], BF16)
    wob = p.sb([128, 8 * 1024], BF16)
    load_w_bf16(ctx, wv, D, 256, wvb)
    load_w_bf16(ctx, wo, D, 1024, wob)
    xnb = p.sb([128, 8 * SLAB], BF16)
    vexp = p.sb([128, 8 * SLAB], BF16)
    for s in range(4):
        xts = [ctx.scrt(i) for i in range(8)]
        p.dma(*[(xts[i], x[i * 128:(i + 1) * 128, s * SLAB:(s + 1) * SLAB]) for i in range(8)])
        xnbts = [xnb[:, i * SLAB:(i + 1) * SLAB] for i in range(8)]
        ln_cm(ctx, xts, SLAB, xnbts)
        for oc in range(2):
            pp = ctx.ps[2][:, 0:SLAB]
            p.mm(pp, [wview(wvb, k, 256)[:, oc * 128:(oc + 1) * 128] for k in range(8)], xnbts)
            fns = []
            for gl in range(2):
                g = 2 * oc + gl
                for t in (2 * g, 2 * g + 1):
                    for half in range(2):
                        fns.append(lambda e, gl=gl, t=t, half=half, pp=pp: e.tensor_copy(
                            vexp[half * 64:half * 64 + 64, t * SLAB:(t + 1) * SLAB],
                            pp[gl * 64:gl * 64 + 64, 0:SLAB]))
            p.g("vector", *fns)
        for oc in range(8):
            pp = ctx.ps[3][:, 0:SLAB]
            p.mm(pp, [wview(wob, t, 1024)[:, oc * 128:(oc + 1) * 128] for t in range(8)],
                 [vexp[:, t * SLAB:(t + 1) * SLAB] for t in range(8)])
            p.g("vector", lambda e, oc=oc, pp=pp: e.tensor_add(ctx.scr2t(oc), pp, ctx.scrt(oc)))
        p.dma(*[(y[i * 128:(i + 1) * 128, s * SLAB:(s + 1) * SLAB], ctx.scr2t(i)) for i in range(8)])
    return p

def build_l2f():
    p = Prog()
    x = p.inp("x", [D, TOWN + 1], F32)
    wr = p.inp("wr", [D, D], F32)
    wk = p.inp("wk", [D, D], F32)
    wvw = p.inp("wv", [D, D], F32)
    mur = p.inp("mur", [128, 8], F32)
    muk = p.inp("muk", [128, 8], F32)
    y = p.outp("y", [D, TOWN], F32)
    ctx = Ctx(p)
    murs = p.sb([128, 8]); muks = p.sb([128, 8])
    p.dma((murs[:, :], mur[:, :]), (muks[:, :], muk[:, :]))
    xrb = p.sb([128, 8 * TOWN], BF16)
    xkb = p.sb([128, 8 * TOWN], BF16)
    rb = p.sb([128, 8 * TOWN], BF16)
    wbuf = p.sb([128, 8 * 1024], BF16)
    lnx = p.sb([128, 8 * 513])
    def cview(t, c, s):
        return t[:, c * TOWN + s * SLAB: c * TOWN + (s + 1) * SLAB]
    def lnxv(c, a, b):
        return lnx[:, c * 513 + a: c * 513 + b]
    p.dma(*[(ctx.scrt(i)[:, 0:1], x[i * 128:(i + 1) * 128, 0:1]) for i in range(8)], allow_slow_non_contiguous=True)
    ln_cm(ctx, [ctx.scrt(i)[:, 0:1] for i in range(8)], 1, [lnxv(i, 0, 1) for i in range(8)])
    for s in range(4):
        xts = [ctx.scrt(i) for i in range(8)]
        p.dma(*[(xts[i], x[i * 128:(i + 1) * 128, 1 + s * SLAB: 1 + (s + 1) * SLAB]) for i in range(8)])
        ln_cm(ctx, xts, SLAB, [lnxv(i, 1, 513) for i in range(8)])
        for c in range(8):
            cur = lnxv(c, 1, 513)
            prev = lnxv(c, 0, 512)
            dtile = ctx.scr2t(c)
            p.g("vector", lambda e, a=cur, b=prev, o=dtile: e.tensor_sub(o, a, b))
            p.g("vector", lambda e, d=dtile, pv=prev, o=cview(xrb, c, s), sc=murs[:, c:c + 1]:
                e.scalar_tensor_tensor(o, d, sc, pv, OP.mult, OP.add))
            p.g("vector", lambda e, d=dtile, pv=prev, o=cview(xkb, c, s), sc=muks[:, c:c + 1]:
                e.scalar_tensor_tensor(o, d, sc, pv, OP.mult, OP.add))
        p.g("vector", *[(lambda e, c=c: e.tensor_copy(lnxv(c, 0, 1), lnxv(c, 512, 513))) for c in range(8)])
    load_w_bf16(ctx, wr, D, 1024, wbuf)
    for s in range(4):
        for oc in range(8):
            pp = ctx.ps[2][:, 0:SLAB]
            p.mm(pp, [wview(wbuf, k, 1024)[:, oc * 128:(oc + 1) * 128] for k in range(8)],
                 [cview(xrb, k, s) for k in range(8)])
            p.act(cview(rb, oc, s), pp, AF.Sigmoid)
    load_w_bf16(ctx, wk, D, 1024, wbuf)
    for s in range(4):
        for oc in range(8):
            pp = ctx.ps[2][:, 0:SLAB]
            p.mm(pp, [wview(wbuf, k, 1024)[:, oc * 128:(oc + 1) * 128] for k in range(8)],
                 [cview(xkb, k, s) for k in range(8)])
            p.g("vector", lambda e, pp=pp: e.tensor_scalar(ctx.scr2t(0), pp, 0.0, None, OP.max))
            p.act(cview(xrb, oc, s), ctx.scr2t(0), AF.Square)
    load_w_bf16(ctx, wvw, D, 1024, wbuf)
    for s in range(4):
        p.dma(*[(ctx.scrt(i), x[i * 128:(i + 1) * 128, 1 + s * SLAB: 1 + (s + 1) * SLAB]) for i in range(8)])
        for oc in range(8):
            pp = ctx.ps[2][:, 0:SLAB]
            p.mm(pp, [wview(wbuf, k, 1024)[:, oc * 128:(oc + 1) * 128] for k in range(8)],
                 [cview(xrb, k, s) for k in range(8)])
            p.g("vector", lambda e, pp=pp, oc=oc, s=s: e.tensor_mul(ctx.scr2t(oc), pp, cview(rb, oc, s)))
            p.g("vector", lambda e, oc=oc: e.tensor_add(ctx.scr2t(oc), ctx.scr2t(oc), ctx.scrt(oc)))
        p.dma(*[(y[i * 128:(i + 1) * 128, s * SLAB:(s + 1) * SLAB], ctx.scr2t(i)) for i in range(8)])
    return p

def build_l3f():
    p = Prog()
    x = p.inp("x", [D, TOWN], F32)
    k0 = p.inp("k0", [2048, 4096], U8)
    k1 = p.inp("k1", [4096, 4096], U8)
    k2 = p.inp("k2", [4096, 1024], U8)
    s0 = p.inp("sc0", [1, 1], F32); s1 = p.inp("sc1", [1, 1], F32); s2 = p.inp("sc2", [1, 1], F32)
    y = p.outp("y", [D, TOWN], F32)
    ctx = Ctx(p, scr_w=4096)
    svals = p.sb([1, 3])
    p.dma((svals[0:1, 0:1], s0[:, :]), (svals[0:1, 1:2], s1[:, :]), (svals[0:1, 2:3], s2[:, :]))
    sc = p.sb([128, 3])
    pb = ctx.ps[0][:, 0:3]
    p.mm(pb, [ctx.ones_row_f[0:1, :]], [svals[0:1, 0:3]])
    p.g("vector", lambda e: e.tensor_copy(sc[:, :], pb))
    s0x4 = p.sb([128, 1]); s1x2 = p.sb([128, 1])
    p.g("vector", lambda e: e.tensor_scalar(s0x4[:, :], sc[:, 0:1], 4.0, None, OP.mult))
    p.g("vector", lambda e: e.tensor_scalar(s1x2[:, :], sc[:, 1:2], 2.0, None, OP.mult))
    xnb = p.sb([128, 16 * SLAB], BF16)
    h0b = p.sb([128, 32 * 1024], BF16)
    h1b = p.sb([128, 32 * 1024], BF16)
    u8a = p.sb([128, 4096], U8)
    u8b = u8a[:, 1024:2048]
    w1b = p.sb([128, 32 * 128], BF16)
    HT = 1024
    for hf in range(2):
        for s in range(2):
            xts = [ctx.scrt(i) for i in range(8)]
            p.dma(*[(xts[i], x[i * 128:(i + 1) * 128, hf * HT + s * SLAB: hf * HT + (s + 1) * SLAB]) for i in range(8)])
            ln_cm(ctx, xts, SLAB, [xnb[:, (s * 8 + i) * SLAB:(s * 8 + i + 1) * SLAB] for i in range(8)])
        def xnt(k, s):
            return xnb[:, (s * 8 + k) * SLAB:(s * 8 + k + 1) * SLAB]
        for oc in range(32):
            p.dma((u8a[:, 0:1024].rearrange("p (t j) -> p t j", j=128),
                   k0[0:1024, oc * 128:(oc + 1) * 128].rearrange("(t p) j -> p t j", p=128)),
                  (u8b.rearrange("p (t j) -> p t j", j=128),
                   k0[1024:2048, oc * 128:(oc + 1) * 128].rearrange("(t p) j -> p t j", p=128)))
            p.g("vector", *[(lambda e, t=t, oc=oc: e.tensor_sub(
                w1b[:, t * 128:(t + 1) * 128], u8a[:, t * 128:(t + 1) * 128], u8b[:, t * 128:(t + 1) * 128]))
                for t in range(8)])
            for s in range(2):
                pp = ctx.ps[1 + s % 2][:, 0:SLAB]
                p.mm(pp, [w1b[:, t * 128:(t + 1) * 128] for t in range(8)], [xnt(t, s) for t in range(8)])
                p.act(h0b[:, (oc * 2 + s) * SLAB:(oc * 2 + s + 1) * SLAB], pp, AF.Relu, scale=s0x4[:, 0:1])
        def h0t(k, s):
            return h0b[:, (k * 2 + s) * SLAB:(k * 2 + s + 1) * SLAB]
        for oc in range(32):
            p.dma(*[(u8a[:, q * 1024:(q + 1) * 1024].rearrange("p (t j) -> p t j", j=128),
                     k1[q * 1024:(q + 1) * 1024, oc * 128:(oc + 1) * 128].rearrange("(t p) j -> p t j", p=128))
                    for q in range(4)])
            p.g("vector", *[(lambda e, t=t: e.tensor_scalar(
                w1b[:, t * 128:(t + 1) * 128], u8a[:, t * 128:(t + 1) * 128], 2.0, -1.0, OP.mult, OP.add))
                for t in range(32)])
            for s in range(2):
                pp = ctx.ps[1 + s % 2][:, 0:SLAB]
                p.mm(pp, [w1b[:, t * 128:(t + 1) * 128] for t in range(32)], [h0t(t, s) for t in range(32)])
                p.act(h1b[:, (oc * 2 + s) * SLAB:(oc * 2 + s + 1) * SLAB], pp, AF.Relu, scale=s1x2[:, 0:1])
        def h1t(k, s):
            return h1b[:, (k * 2 + s) * SLAB:(k * 2 + s + 1) * SLAB]
        for oc in range(8):
            p.dma(*[(u8a[:, q * 1024:(q + 1) * 1024].rearrange("p (t j) -> p t j", j=128),
                     k2[q * 1024:(q + 1) * 1024, oc * 128:(oc + 1) * 128].rearrange("(t p) j -> p t j", p=128))
                    for q in range(4)])
            p.g("vector", *[(lambda e, t=t: e.tensor_scalar(
                w1b[:, t * 128:(t + 1) * 128], u8a[:, t * 128:(t + 1) * 128], 2.0, -1.0, OP.mult, OP.add))
                for t in range(32)])
            for s in range(2):
                pp = ctx.ps[1 + s % 2][:, 0:SLAB]
                p.mm(pp, [w1b[:, t * 128:(t + 1) * 128] for t in range(32)], [h1t(t, s) for t in range(32)])
                p.dma((ctx.scrt(0), x[oc * 128:(oc + 1) * 128, hf * HT + s * SLAB: hf * HT + (s + 1) * SLAB]))
                p.g("vector", lambda e, pp=pp: e.scalar_tensor_tensor(
                    ctx.scr2t(0), pp, sc[:, 2:3], ctx.scrt(0), OP.mult, OP.add))
                p.dma((y[oc * 128:(oc + 1) * 128, hf * HT + s * SLAB: hf * HT + (s + 1) * SLAB], ctx.scr2t(0)))
    return p


def build_l2ag():
    p = Prog()
    x = p.inp("x", [D, N], F32)
    xq = p.inp("xq", [D, TOWN], F32)
    wq = p.inp("wq", [D, D], F32)
    kvd = p.inp("kvd", [D, 256], F32)
    kup = p.inp("kup", [256, 256], F32)
    vup = p.inp("vup", [256, 256], F32)
    wo = p.inp("wo", [D, D], F32)
    n1w = p.inp("n1w", [128, 8], F32)
    masks = p.inp("masks", [8 * 4 * 128, 256], BF16)
    qcos = p.inp("qcos", [128, TOWN], BF16)
    qsin = p.inp("qsin", [128, TOWN], BF16)
    kcos = p.inp("kcos", [128, N], BF16)
    ksin = p.inp("ksin", [128, N], BF16)
    y = p.outp("y", [D, TOWN], F32)
    ctx = Ctx(p, nps=2, scr_w=4096)
    psS = p.ps([128, 1024])
    psO = p.ps([64, 256])
    psP = p.ps([128, 512])
    maskb = p.sb([128, 4 * 256], BF16)
    tabq = p.sb([128, 2 * TOWN], BF16); tabk = p.sb([128, 2 * N], BF16)
    p.dma((tabq[:, 0:TOWN], qcos[:, :]), (tabq[:, TOWN:], qsin[:, :]),
          (tabk[:, 0:N], kcos[:, :]), (tabk[:, N:], ksin[:, :]))
    n1s = p.sb([128, 8])
    p.dma((n1s[:, :], n1w[:, :]))
    xn2s = p.sb([128, 8 * SLAB], BF16)
    latT = [p.sb([128, N], BF16) for _ in range(2)]
    kthr = [p.sb([128, N], BF16) for _ in range(4)]
    vtb = p.sb([128, 32 * 256], BF16)
    qth = latT[0][:, 0:2048]
    otblk = latT[0][:, 2048:4096]
    lnqb = p.sb([128, 8 * 256])
    pbuf = p.sb([128, 32 * 256], BF16)
    wqb = p.sb([128, 8 * 1024], BF16)
    wob = p.sb([128, 8 * 1024], BF16)
    kvdb = p.sb([128, 8 * 256], BF16)
    kupb = p.sb([128, 2 * 256], BF16)
    vupb = p.sb([128, 2 * 256], BF16)
    fbuf = p.sb([128, 512])
    otmp = p.sb([64, 256], BF16)
    load_w_bf16(ctx, wq, D, 1024, wqb)
    load_w_bf16(ctx, wo, D, 1024, wob)
    load_w_bf16(ctx, kvd, D, 256, kvdb)
    load_w_bf16(ctx, kup, 256, 256, kupb)
    load_w_bf16(ctx, vup, 256, 256, vupb)
    VM2 = 1.0 + 1e-6
    VA2 = 1e-5 * 1e-6
    zb = None
    for s in range(8):
        xts = [ctx.scrt(i) for i in range(8)]
        p.dma(*[(xts[i], x[i * 128:(i + 1) * 128, s * SLAB:(s + 1) * SLAB]) for i in range(8)])
        xns = [xn2s[:, i * SLAB:(i + 1) * SLAB] for i in range(8)]
        if zb is None:
            zb = p.ctile(0.0)
        ln_cm(ctx, xts, SLAB, xns, eps=VA2, var_mul=VM2,
              gpp=[n1s[:, i:i + 1] for i in range(8)], bpp=[zb] * 8)
        for dc in range(2):
            p.mm(psP[:, 0:SLAB], [wview(kvdb, k, 256)[:, dc * 128:(dc + 1) * 128] for k in range(8)], xns)
            p.g("vector", lambda e, dc=dc, s=s: e.tensor_copy(latT[dc][:, s * SLAB:(s + 1) * SLAB], psP[:, 0:SLAB]))
    for s in range(8):
        lts = [latT[k][:, s * SLAB:(s + 1) * SLAB] for k in range(2)]
        for dc in range(2):
            p.mm(psP[:, 0:SLAB], [wview(kupb, k, 256)[:, dc * 128:(dc + 1) * 128] for k in range(2)], lts)
            def dst2(h2, dc=dc, s=s):
                g = 2 * dc + h2
                return kthr[g][h2 * 64:h2 * 64 + 64, s * SLAB:(s + 1) * SLAB]
            rope_block(ctx, psP[:, 0:SLAB], fbuf[:, 0:SLAB],
                       tabk[:, s * SLAB:(s + 1) * SLAB], tabk[:, N + s * SLAB:N + (s + 1) * SLAB],
                       SLAB, dst2, qknorm_eps=1e-6)
            p.g("vector", *[(lambda e, h2=h2, dc=dc, s=s: e.tensor_copy(
                kthr[2 * dc + h2][(1 - h2) * 64:(1 - h2) * 64 + 64, s * SLAB:(s + 1) * SLAB],
                kthr[2 * dc + h2][h2 * 64:h2 * 64 + 64, s * SLAB:(s + 1) * SLAB])) for h2 in range(2)])
        for tt in range(4):
            p.mm(psP[:, 0:256], [latT[k][:, s * SLAB + tt * 128:s * SLAB + (tt + 1) * 128] for k in range(2)],
                 [wview(vupb, k, 256) for k in range(2)])
            p.g("vector", lambda e, s=s, tt=tt: e.tensor_copy(vtb[:, (s * 4 + tt) * 256:(s * 4 + tt + 1) * 256], psP[:, 0:256]))
    for m in range(8):
        xts = [ctx.scrt(i, 256) for i in range(8)]
        p.dma(*[(xts[i], xq[i * 128:(i + 1) * 128, m * 256:(m + 1) * 256]) for i in range(8)])
        p.dma(*[(maskb[:, j * 256:(j + 1) * 256], masks[(m * 4 + j) * 128:(m * 4 + j + 1) * 128, :]) for j in range(4)])
        lns = [lnqb[:, i * 256:(i + 1) * 256] for i in range(8)]
        qs2 = [xn2s[:, i * 256:(i + 1) * 256] for i in range(8)]
        ln_cm(ctx, xts, 256, lns, second=(qs2, VM2, VA2, [n1s[:, i:i + 1] for i in range(8)]))
        for dc in range(8):
            p.mm(psP[:, 0:256], [wview(wqb, k, 1024)[:, dc * 128:(dc + 1) * 128] for k in range(8)], qs2)
            rope_block(ctx, psP[:, 0:256], fbuf[:, 0:256],
                       tabq[:, m * 256:(m + 1) * 256], tabq[:, TOWN + m * 256:TOWN + (m + 1) * 256],
                       256, lambda h2, dc=dc: qth[h2 * 64:h2 * 64 + 64, dc * 256:(dc + 1) * 256],
                       qknorm_eps=1e-6)
        nch = 4 * m + 4
        for h in range(16):
            g = h // 4
            b = (h % 2) * 64
            bcs = attn_unit(
                ctx, nch,
                lambda c, g=g, b=b: kthr[g][b:b + 64, c * 128:(c + 1) * 128],
                qth[b:b + 64, (h // 2) * 256:(h // 2) * 256 + 256],
                lambda c, g=g: vtb[:, c * 256 + g * 64: c * 256 + g * 64 + 64],
                psS, pbuf, psO, 256, 0.125,
                mask_fn=lambda c, nch=nch: (maskb[:, (c - (nch - 4)) * 256:(c - (nch - 4) + 1) * 256]
                                                 if c >= nch - 4 else None))
            p.g("vector", lambda e, bcs=bcs: e.tensor_mul(otmp[0:64, 0:256], psO[0:64, 0:256], bcs))
            ot = otblk[(h % 2) * 64:(h % 2) * 64 + 64, (h // 2) * 256:(h // 2) * 256 + 256]
            p.g("vector", lambda e, ot=ot: e.tensor_copy(ot, otmp[0:64, 0:256]))
        for oc in range(8):
            p.mm(psP[:, 0:256], [wview(wob, k, 1024)[:, oc * 128:(oc + 1) * 128] for k in range(8)],
                 [otblk[:, k * 256:(k + 1) * 256] for k in range(8)])
            p.g("vector", lambda e, oc=oc: e.tensor_add(ctx.scr2t(oc, 256), psP[:, 0:256], lnqb[:, oc * 256:(oc + 1) * 256]))
        p.dma(*[(y[i * 128:(i + 1) * 128, m * 256:(m + 1) * 256], ctx.scr2t(i, 256)) for i in range(8)])
    return p

def build_l2al():
    KL = 2304
    p = Prog()
    x = p.inp("x", [D, KL], F32)
    xo = p.inp("xo", [D, TOWN], F32)
    wq = p.inp("wq", [D, D], F32)
    wk = p.inp("wk", [D, 256], F32)
    wv = p.inp("wv", [D, 256], F32)
    wo = p.inp("wo", [D, D], F32)
    n2w = p.inp("n2w", [128, 8], F32)
    masks = p.inp("masks", [8 * 4 * 128, 256], BF16)
    qcos = p.inp("qcos", [128, TOWN], BF16)
    qsin = p.inp("qsin", [128, TOWN], BF16)
    kcos = p.inp("kcos", [128, KL], BF16)
    ksin = p.inp("ksin", [128, KL], BF16)
    y = p.outp("y", [D, TOWN], F32)
    ctx = Ctx(p, nps=2, scr_w=4096)
    psS = p.ps([128, 1024])
    psO = p.ps([64, 256])
    psP = p.ps([128, 512])
    maskb = p.sb([128, 32 * 256], BF16)
    p.dma(*[(maskb[:, i * 256:(i + 1) * 256], masks[i * 128:(i + 1) * 128, :]) for i in range(32)])
    tabq = p.sb([128, 2 * TOWN], BF16); tabk = p.sb([128, 2 * KL], BF16)
    p.dma((tabq[:, 0:TOWN], qcos[:, :]), (tabq[:, TOWN:], qsin[:, :]),
          (tabk[:, 0:KL], kcos[:, :]), (tabk[:, KL:], ksin[:, :]))
    n2s = p.sb([128, 8])
    p.dma((n2s[:, :], n2w[:, :]))
    xn2s = p.sb([128, 8 * SLAB], BF16)
    kthr = [p.sb([128, KL], BF16) for _ in range(4)]
    vtb = p.sb([128, 18 * 256], BF16)
    qth = p.sb([128, 8 * 256], BF16)
    otblk = p.sb([128, 8 * 256], BF16)
    pbuf = p.sb([128, 4 * 256], BF16)
    wqb = p.sb([128, 8 * 1024], BF16)
    wob = p.sb([128, 8 * 1024], BF16)
    wkb = p.sb([128, 8 * 256], BF16)
    wvb = p.sb([128, 8 * 256], BF16)
    fbuf = p.sb([128, 512])
    otmp = p.sb([64, 256], BF16)
    load_w_bf16(ctx, wq, D, 1024, wqb)
    load_w_bf16(ctx, wo, D, 1024, wob)
    load_w_bf16(ctx, wk, D, 256, wkb)
    load_w_bf16(ctx, wv, D, 256, wvb)
    widths = [512, 512, 512, 512, 256]
    col0 = 0
    for s, W in enumerate(widths):
        xts = [ctx.scrt(i)[:, 0:W] for i in range(8)]
        p.dma(*[(xts[i], x[i * 128:(i + 1) * 128, col0:col0 + W]) for i in range(8)])
        xns = [xn2s[:, i * SLAB:i * SLAB + W] for i in range(8)]
        ln_cm(ctx, xts, W, xns, eps=1e-6, rms=True, wpp=[n2s[:, i:i + 1] for i in range(8)])
        for dc in range(2):
            p.mm(psP[:, 0:W], [wview(wkb, k, 256)[:, dc * 128:(dc + 1) * 128] for k in range(8)], xns)
            def dst2(h2, dc=dc, col0=col0, W=W):
                g = 2 * dc + h2
                return kthr[g][h2 * 64:h2 * 64 + 64, col0:col0 + W]
            rope_block(ctx, psP[:, 0:W], fbuf[:, 0:W],
                       tabk[:, col0:col0 + W], tabk[:, KL + col0:KL + col0 + W],
                       W, dst2, qknorm_eps=1e-6)
            p.g("vector", *[(lambda e, h2=h2, dc=dc, col0=col0, W=W: e.tensor_copy(
                kthr[2 * dc + h2][(1 - h2) * 64:(1 - h2) * 64 + 64, col0:col0 + W],
                kthr[2 * dc + h2][h2 * 64:h2 * 64 + 64, col0:col0 + W])) for h2 in range(2)])
        for tt in range(W // 128):
            p.mm(psP[:, 0:256], [xns[k][:, tt * 128:(tt + 1) * 128] for k in range(8)],
                 [wview(wvb, k, 256) for k in range(8)])
            ti = col0 // 128 + tt
            p.g("vector", lambda e, ti=ti: e.tensor_copy(vtb[:, ti * 256:(ti + 1) * 256], psP[:, 0:256]))
        col0 += W
    for m in range(8):
        xts = [ctx.scrt(i, 256) for i in range(8)]
        p.dma(*[(xts[i], x[i * 128:(i + 1) * 128, 256 + m * 256:256 + (m + 1) * 256]) for i in range(8)])
        qs2 = [xn2s[:, i * 256:(i + 1) * 256] for i in range(8)]
        ln_cm(ctx, xts, 256, qs2, eps=1e-6, rms=True, wpp=[n2s[:, i:i + 1] for i in range(8)])
        for dc in range(8):
            p.mm(psP[:, 0:256], [wview(wqb, k, 1024)[:, dc * 128:(dc + 1) * 128] for k in range(8)], qs2)
            rope_block(ctx, psP[:, 0:256], fbuf[:, 0:256],
                       tabq[:, m * 256:(m + 1) * 256], tabq[:, TOWN + m * 256:TOWN + (m + 1) * 256],
                       256, lambda h2, dc=dc: qth[h2 * 64:h2 * 64 + 64, dc * 256:(dc + 1) * 256],
                       qknorm_eps=1e-6)
        for h in range(16):
            g = h // 4
            b = (h % 2) * 64
            bcs = attn_unit(
                ctx, 4,
                lambda c, g=g, b=b, m=m: kthr[g][b:b + 64, m * 256 + c * 128: m * 256 + (c + 1) * 128],
                qth[b:b + 64, (h // 2) * 256:(h // 2) * 256 + 256],
                lambda c, g=g, m=m: vtb[:, (m * 2 + c) * 256 + g * 64: (m * 2 + c) * 256 + g * 64 + 64],
                psS, pbuf, psO, 256, 0.125,
                mask_fn=lambda c, m=m: maskb[:, (m * 4 + c) * 256:(m * 4 + c + 1) * 256])
            p.g("vector", lambda e, bcs=bcs: e.tensor_mul(otmp[0:64, 0:256], psO[0:64, 0:256], bcs))
            ot = otblk[(h % 2) * 64:(h % 2) * 64 + 64, (h // 2) * 256:(h // 2) * 256 + 256]
            p.g("vector", lambda e, ot=ot: e.tensor_copy(ot, otmp[0:64, 0:256]))
        p.dma(*[(ctx.scrt(i, 256), x[i * 128:(i + 1) * 128, 256 + m * 256:256 + (m + 1) * 256]) for i in range(8)])
        p.dma(*[(ctx.scr2t(i, 256), xo[i * 128:(i + 1) * 128, m * 256:(m + 1) * 256]) for i in range(8)])
        for oc in range(8):
            p.mm(psP[:, 0:256], [wview(wob, k, 1024)[:, oc * 128:(oc + 1) * 128] for k in range(8)],
                 [otblk[:, k * 256:(k + 1) * 256] for k in range(8)])
            p.g("vector", lambda e, oc=oc: e.tensor_add(ctx.scrt(oc, 256), psP[:, 0:256], ctx.scrt(oc, 256)))
            p.g("vector", lambda e, oc=oc: e.tensor_add(ctx.scrt(oc, 256), ctx.scrt(oc, 256), ctx.scr2t(oc, 256)))
        p.dma(*[(y[i * 128:(i + 1) * 128, m * 256:(m + 1) * 256], ctx.scrt(i, 256)) for i in range(8)])
    return p


def build_l1a():
    p = Prog()
    x = p.inp("x", [D, N], F32)
    xq = p.inp("xq", [D, TOWN], F32)
    wqkv = p.inp("wqkv", [D, 3 * D], F32)
    wproj = p.inp("wproj", [D, D], F32)
    bproj = p.inp("bproj", [128, 8], F32)
    lq1 = p.inp("lq1", [1, 32], F32); lk1 = p.inp("lk1", [1, 32], F32)
    lq2 = p.inp("lq2", [1, 32], F32); lk2 = p.inp("lk2", [1, 32], F32)
    subw = p.inp("subw", [64, 1], F32)
    y = p.outp("y", [D, TOWN], F32)
    ctx = Ctx(p, nps=2, scr_w=4096)
    psS = p.ps([128, 1024])
    psO1 = p.ps([64, 512])
    psO2 = p.ps([64, 512])
    psP = p.ps([128, 512])
    NQ = 512
    kthA = [p.sb([64, N], BF16) for _ in range(2)]
    kthB = [p.sb([64, N], BF16) for _ in range(2)]
    vtb = p.sb([128, 32 * 256], BF16)
    qthA = [p.sb([64, NQ], BF16) for _ in range(2)]
    qthB = [p.sb([64, NQ], BF16) for _ in range(2)]
    otb = p.sb([128, 8 * TOWN], BF16)
    pbuf = p.sb([128, 32 * NQ], BF16)
    qsrc = p.sb([128, 8 * NQ], BF16)
    xnsb = p.sb([128, 8 * SLAB], BF16)
    wb = p.sb([128, 8 * 256], BF16)
    wb2 = p.sb([128, 8 * 256], BF16)
    wpb = p.sb([128, 8 * 1024], BF16)
    lamv = p.sb([1, 64])
    lam64 = p.sb([64, NQ])
    t1 = p.sb([64, NQ]); t2 = p.sb([64, NQ])
    sw8 = p.sb([64, 1])
    otmp = p.sb([64, NQ], BF16)
    p.dma((lamv[0:1, 0:32], lq1[:, :]), (lamv[0:1, 32:64], lk1[:, :]), (sw8[:, :], subw[:, :]))
    p.g("vector", lambda e: e.tensor_mul(lamv[0:1, 0:32], lamv[0:1, 0:32], lamv[0:1, 32:64]))
    p.g("vector", lambda e: e.reduce_sum(ctx.rows[0:1, 0:1], lamv[0:1, 0:32], axis=AX.X))
    p.act(ctx.rows[0:1, 1:2], ctx.rows[0:1, 0:1], AF.Exp)
    p.dma((lamv[0:1, 0:32], lq2[:, :]), (lamv[0:1, 32:64], lk2[:, :]))
    p.g("vector", lambda e: e.tensor_mul(lamv[0:1, 0:32], lamv[0:1, 0:32], lamv[0:1, 32:64]))
    p.g("vector", lambda e: e.reduce_sum(ctx.rows[0:1, 0:1], lamv[0:1, 0:32], axis=AX.X))
    p.act(ctx.rows[0:1, 2:3], ctx.rows[0:1, 0:1], AF.Exp)
    p.g("vector", lambda e: e.tensor_sub(ctx.rows[0:1, 3:4], ctx.rows[0:1, 1:2], ctx.rows[0:1, 2:3]))
    p.g("vector", lambda e: e.tensor_scalar(ctx.rows[0:1, 3:4], ctx.rows[0:1, 3:4], 0.2, None, OP.add))
    p.g("vector", lambda e: e.memset(ctx.rows2[0:1, 0:NQ], 1.0))
    p.mm(ctx.ps1[0:1, 0:NQ], [ctx.rows[0:1, 3:4]], [ctx.rows2[0:1, 0:NQ]])
    p.g("vector", lambda e: e.tensor_copy(ctx.rows[0:1, 512:512 + NQ], ctx.ps1[0:1, 0:NQ]))
    p.mm(ctx.ps[0][0:64, 0:NQ], [ctx.ones_row_f[0:1, 0:64]], [ctx.rows[0:1, 512:512 + NQ]])
    p.g("vector", lambda e: e.tensor_copy(lam64[:, :], ctx.ps[0][0:64, 0:NQ]))
    p.g("vector", lambda e: e.tensor_scalar(sw8[:, :], sw8[:, :], 0.8, None, OP.mult))
    for q4 in range(4):
        load_w_bf16(ctx, wqkv[:, 1024 + q4 * 256: 1024 + (q4 + 1) * 256], D, 256, wb)
        load_w_bf16(ctx, wqkv[:, 2048 + q4 * 256: 2048 + (q4 + 1) * 256], D, 256, wb2)
        for s in range(8):
            xts = [ctx.scrt(i) for i in range(8)]
            p.dma(*[(xts[i], x[i * 128:(i + 1) * 128, s * SLAB:(s + 1) * SLAB]) for i in range(8)])
            xns = [xnsb[:, i * SLAB:(i + 1) * SLAB] for i in range(8)]
            ln_cm(ctx, xts, SLAB, xns)
            for dc in range(2):
                p.mm(psP[:, 0:SLAB], [wview(wb, k, 256)[:, dc * 128:(dc + 1) * 128] for k in range(8)], xns)
                p.g("vector",
                    lambda e, dc=dc, s=s: e.tensor_copy(kthA[dc][:, s * SLAB:(s + 1) * SLAB], psP[0:64, 0:SLAB]),
                    lambda e, dc=dc, s=s: e.tensor_copy(kthB[dc][:, s * SLAB:(s + 1) * SLAB], psP[64:128, 0:SLAB]))
            for tt in range(4):
                p.mm(psP[:, 0:256], [xns[k][:, tt * 128:(tt + 1) * 128] for k in range(8)],
                     [wview(wb2, k, 256) for k in range(8)])
                p.g("vector", lambda e, s=s, tt=tt: e.tensor_copy(vtb[:, (s * 4 + tt) * 256:(s * 4 + tt + 1) * 256], psP[:, 0:256]))
        load_w_bf16(ctx, wqkv[:, q4 * 256:(q4 + 1) * 256], D, 256, wb)
        for m in range(4):
            xts = [ctx.scrt(i) for i in range(8)]
            p.dma(*[(xts[i], xq[i * 128:(i + 1) * 128, m * NQ:(m + 1) * NQ]) for i in range(8)])
            qs = [qsrc[:, i * NQ:(i + 1) * NQ] for i in range(8)]
            ln_cm(ctx, xts, NQ, qs)
            for dc in range(2):
                p.mm(psP[:, 0:NQ], [wview(wb, k, 256)[:, dc * 128:(dc + 1) * 128] for k in range(8)], qs)
                p.g("vector",
                    lambda e, dc=dc: e.tensor_copy(qthA[dc][:, 0:NQ], psP[0:64, 0:NQ]),
                    lambda e, dc=dc: e.tensor_copy(qthB[dc][:, 0:NQ], psP[64:128, 0:NQ]))
            for hq in range(4):
                h = q4 * 4 + hq
                t = hq // 2
                j = hq % 2
                bcs1 = attn_unit(
                    ctx, 32,
                    lambda c, t=t, j=j: kthA[t][j * 32:(j + 1) * 32, c * 128:(c + 1) * 128],
                    qthA[t][j * 32:(j + 1) * 32, 0:NQ],
                    lambda c, hq=hq: vtb[:, c * 256 + hq * 64: c * 256 + hq * 64 + 64],
                    psS, pbuf, psO1, NQ, 0.1767766952966369,
                    pk=128)
                p.g("vector", lambda e, bcs1=bcs1: e.tensor_mul(t1[:, :], psO1[0:64, 0:NQ], bcs1))
                bcs2 = attn_unit(
                    ctx, 32,
                    lambda c, t=t, j=j: kthB[t][j * 32:(j + 1) * 32, c * 128:(c + 1) * 128],
                    qthB[t][j * 32:(j + 1) * 32, 0:NQ],
                    lambda c, hq=hq: vtb[:, c * 256 + hq * 64: c * 256 + hq * 64 + 64],
                    psS, pbuf, psO2, NQ, 0.1767766952966369,
                    pk=128)
                p.g("vector", lambda e, bcs2=bcs2: e.tensor_mul(t2[:, :], psO2[0:64, 0:NQ], bcs2))
                p.g("vector", lambda e: e.tensor_mul(t2[:, :], t2[:, :], lam64[:, :]))
                p.g("vector", lambda e: e.tensor_sub(t1[:, :], t1[:, :], t2[:, :]))
                p.act(t2[:, :], t1[:, :], AF.Square)
                p.mm(ctx.ps1[0:1, 0:NQ], [ctx.ones_col_f[0:64, 0:1]], [t2[:, :]])
                p.g("vector", lambda e: e.tensor_scalar(ctx.rows[0:1, 0:NQ], ctx.ps1[0:1, 0:NQ], 1.0 / 64.0, None, OP.mult))
                p.act(ctx.rows[0:1, 512:512 + NQ], ctx.rows[0:1, 0:NQ], AF.Sqrt, bias=1e-5)
                p.g("vector", lambda e: e.reciprocal(ctx.rows2[0:1, 0:NQ], ctx.rows[0:1, 512:512 + NQ]))
                p.mm(ctx.ps[0][0:64, 0:NQ], [ctx.ones_row_f[0:1, 0:64]], [ctx.rows2[0:1, 0:NQ]])
                p.g("vector", lambda e: e.tensor_mul(t1[:, :], t1[:, :], ctx.ps[0][0:64, 0:NQ]))
                p.g("vector", lambda e: e.tensor_scalar(otmp[:, :], t1[:, :], sw8[:, 0:1], None, OP.mult))
                ot = otb[(h % 2) * 64:(h % 2) * 64 + 64, (h // 2) * TOWN + m * NQ:(h // 2) * TOWN + (m + 1) * NQ]
                p.g("vector", lambda e, ot=ot: e.tensor_copy(ot, otmp[:, :]))
    load_w_bf16(ctx, wproj, D, 1024, wpb)
    bps = p.sb([128, 8])
    p.dma((bps[:, :], bproj[:, :]))
    for s in range(4):
        p.dma(*[(ctx.scrt(i), xq[i * 128:(i + 1) * 128, s * SLAB:(s + 1) * SLAB]) for i in range(8)])
        for oc in range(8):
            p.mm(psP[:, 0:SLAB], [wview(wpb, k, 1024)[:, oc * 128:(oc + 1) * 128] for k in range(8)],
                 [otb[:, k * TOWN + s * SLAB: k * TOWN + (s + 1) * SLAB] for k in range(8)])
            p.g("vector", lambda e, oc=oc: e.tensor_scalar(ctx.scr2t(oc), psP[:, 0:SLAB], bps[:, oc:oc + 1], None, OP.add))
            p.g("vector", lambda e, oc=oc: e.tensor_add(ctx.scr2t(oc), ctx.scr2t(oc), ctx.scrt(oc)))
        p.dma(*[(y[i * 128:(i + 1) * 128, s * SLAB:(s + 1) * SLAB], ctx.scr2t(i)) for i in range(8)])
    return p

def build_l1f():
    p = Prog()
    x = p.inp("x", [D, TOWN], F32)
    wg = p.inp("wg", [D, 16], F32)
    w1e = p.inp("w1e", [16 * D, 512], F32)
    w3e = p.inp("w3e", [16 * D, 512], F32)
    w2e = p.inp("w2e", [16 * 512, D], F32)
    w1s = p.inp("w1s", [D, D], F32)
    w3s = p.inp("w3s", [D, D], F32)
    w2s = p.inp("w2s", [D, D], F32)
    y = p.outp("y", [D, TOWN], F32)
    ctx = Ctx(p, nps=2, scr_w=4096)
    psP = p.ps([128, 512])
    psL = p.ps([128, 16])
    psT = p.ps([16, 128])
    xnb = p.sb([128, 8 * TOWN], BF16)
    acc = p.sb([128, 8 * TOWN])
    gbuf = p.sb([128, 4 * TOWN], BF16)
    hbuf = p.sb([128, 4 * TOWN], BF16)
    wba = p.sb([128, 8 * 512], BF16)
    wbb = p.sb([128, 8 * 512], BF16)
    combT = p.sb([16, TOWN])
    sig = p.sb([128, 16 * 16])
    gsc = p.sb([128, 8])
    sel = p.sb([128, 16])
    idr = p.sb([128, 128])
    ident = p.const(np.eye(128, dtype=np.float32), "ident128")
    p.dma((idr[:, :], ident[:, :]))
    crow = p.sb([1, TOWN])
    combD = p.nc.dram_tensor("combd", [16, TOWN], F32, kind="Internal").ap()
    wgf = p.sb([128, 8 * 16])
    p.dma(*[(wgf[:, k * 16:(k + 1) * 16], wg[k * 128:(k + 1) * 128, :]) for k in range(8)])
    def xnt(c, s):
        return xnb[:, c * TOWN + s * SLAB: c * TOWN + (s + 1) * SLAB]
    for s in range(4):
        xts = [ctx.scrt(i) for i in range(8)]
        p.dma(*[(xts[i], x[i * 128:(i + 1) * 128, s * SLAB:(s + 1) * SLAB]) for i in range(8)])
        xns = [ctx.scr2t(i) for i in range(8)]
        ln_cm(ctx, xts, SLAB, xns)
        for c in range(8):
            p.g("vector", lambda e, c=c, s=s: e.tensor_copy(xnt(c, s), ctx.scr2t(c)))
        for tt in range(4):
            ti = s * 4 + tt
            p.mm(psL[:, 0:16], [ctx.scr2t(k)[:, tt * 128:(tt + 1) * 128] for k in range(8)],
                 [wgf[:, k * 16:(k + 1) * 16] for k in range(8)])
            p.act(sig[:, ti * 16:(ti + 1) * 16], psL[:, 0:16], AF.Sigmoid)
    m1 = gsc[:, 0:4]; m2 = gsc[:, 4:8]
    ebias = p.const(np.tile((np.arange(16, dtype=np.float32) * -1e-6)[None, :], (128, 1)), "ebias")
    ebs = p.sb([128, 16])
    p.dma((ebs[:, :], ebias[:, :]))
    ones4 = gsc[:, 0:4]
    ones4 = p.sb([128, 4])
    p.g("vector", lambda e: e.memset(ones4[:, :], 1.0))
    for ti in range(16):
        sg = sig[:, ti * 16:(ti + 1) * 16]
        sgb = ctx.scr[:, 64:80]
        cand = ctx.scr[:, 0:16]
        eqm = ctx.scr[:, 16:32]
        gs = ctx.scr[:, 32:36]
        gm2 = ctx.scr[:, 36:38]
        allow16 = ctx.scr[:, 48:64]
        p.g("vector", lambda e, sg=sg: e.tensor_add(sgb[:, :], sg, ebs[:, :]))
        sg3 = sgb.rearrange("p (g k) -> p g k", k=4)
        p.g("vector", lambda e: e.reduce_max(m1[:, :], sg3, axis=AX.X))
        for g in range(4):
            p.g("vector", lambda e, g=g: e.tensor_scalar(eqm[:, g * 4:(g + 1) * 4], sgb[:, g * 4:(g + 1) * 4], m1[:, g:g + 1], None, OP.is_equal))
        p.g("vector", lambda e: e.scalar_tensor_tensor(cand[:, :], eqm[:, :], -1e30, sgb, OP.mult, OP.add))
        p.g("vector", lambda e: e.reduce_max(m2[:, :], cand.rearrange("p (g k) -> p g k", k=4), axis=AX.X))
        p.g("vector", lambda e: e.tensor_add(gs[:, :], m1[:, :], m2[:, :]))
        p.g("vector", lambda e: e.reduce_max(gm2[:, 0:1], gs[:, :], axis=AX.X))
        p.g("vector", lambda e: e.tensor_scalar(eqm[:, 0:4], gs[:, :], gm2[:, 0:1], None, OP.is_equal))
        p.g("vector", lambda e: e.scalar_tensor_tensor(eqm[:, 4:8], eqm[:, 0:4], -1e30, gs, OP.mult, OP.add))
        p.g("vector", lambda e: e.reduce_max(gm2[:, 0:1], eqm[:, 4:8], axis=AX.X))
        p.g("vector", lambda e: e.tensor_scalar(eqm[:, 0:4], gs[:, :], gm2[:, 0:1], None, OP.is_ge))
        for g in range(4):
            p.g("vector", lambda e, g=g: e.tensor_scalar(allow16[:, g * 4:(g + 1) * 4], ones4[:, :], eqm[:, g:g + 1], None, OP.mult))
        p.g("vector", lambda e: e.tensor_scalar(allow16[:, :], allow16[:, :], 1000.0, -1000.0, OP.mult, OP.add))
        p.g("vector", lambda e: e.tensor_add(cand[:, :], allow16[:, :], sgb))
        selacc = sel
        p.g("vector", lambda e: e.memset(selacc[:, :], 0.0))
        for it in range(4):
            p.g("vector", lambda e: e.reduce_max(gm2[:, 0:1], cand[:, :], axis=AX.X))
            p.g("vector", lambda e: e.tensor_scalar(eqm[:, :], cand[:, :], gm2[:, 0:1], None, OP.is_equal))
            p.g("vector", lambda e: e.tensor_add(selacc[:, :], selacc[:, :], eqm[:, :]))
            p.g("vector", lambda e: e.scalar_tensor_tensor(cand[:, :], eqm[:, :], -1e30, cand, OP.mult, OP.add))
        p.g("vector", lambda e, sg=sg: e.tensor_mul(selacc[:, :], selacc[:, :], sg))
        p.g("vector", lambda e: e.reduce_sum(gm2[:, 0:1], selacc[:, :], axis=AX.X))
        p.g("vector", lambda e: e.tensor_scalar(gm2[:, 0:1], gm2[:, 0:1], 1e-6, None, OP.add))
        p.g("vector", lambda e: e.reciprocal(gm2[:, 1:2], gm2[:, 0:1]))
        p.g("vector", lambda e: e.tensor_scalar(selacc[:, :], selacc[:, :], gm2[:, 1:2], None, OP.mult))
        p.g("tensor", lambda e: e.transpose(psT[:, :], selacc[:, :], idr[:, :]))
        p.g("vector", lambda e, ti=ti: e.tensor_copy(combT[:, ti * 128:(ti + 1) * 128], psT[0:16, 0:128]))
    p.dma((combD[:, :], combT[:, :]))
    cdbg = p.outp("comb", [16, TOWN], F32)
    p.dma((cdbg[:, :], combT[:, :]))
    for c in range(8):
        p.g("vector", lambda e, c=c: e.memset(acc[:, c * TOWN:(c + 1) * TOWN], 0.0))
    for ex in range(18):
        if ex < 16:
            load_w_bf16(ctx, w1e[ex * D:(ex + 1) * D, :], D, 512, wba)
            load_w_bf16(ctx, w3e[ex * D:(ex + 1) * D, :], D, 512, wbb)
        else:
            j = ex - 16
            load_w_bf16(ctx, w1s[:, j * 512:(j + 1) * 512], D, 512, wba)
            load_w_bf16(ctx, w3s[:, j * 512:(j + 1) * 512], D, 512, wbb)
        FO = 512
        nfc = 4
        if ex == 16:
            pass
        for s in range(4):
            for fc in range(nfc):
                p.mm(psP[:, 0:SLAB], [wview(wba, k, FO)[:, fc * 128:(fc + 1) * 128] for k in range(8)],
                     [xnt(k, s) for k in range(8)])
                p.act(hbuf[:, (fc * 4 + s) * SLAB:(fc * 4 + s + 1) * SLAB], psP[:, 0:SLAB], AF.Silu)
            for fc in range(nfc):
                p.mm(psP[:, 0:SLAB], [wview(wbb, k, FO)[:, fc * 128:(fc + 1) * 128] for k in range(8)],
                     [xnt(k, s) for k in range(8)])
                p.g("vector", lambda e, fc=fc, s=s: e.tensor_mul(
                    gbuf[:, (fc * 4 + s) * SLAB:(fc * 4 + s + 1) * SLAB],
                    hbuf[:, (fc * 4 + s) * SLAB:(fc * 4 + s + 1) * SLAB], psP[:, 0:SLAB]))
            if ex < 16:
                if s == 0:
                    p.dma((crow[0:1, :], combD[ex:ex + 1, :]))
                p.mm(ctx.ps[0][:, 0:SLAB], [ctx.ones_row_f[0:1, :]], [crow[0:1, s * SLAB:(s + 1) * SLAB]])
                p.g("vector", lambda e: e.tensor_copy(ctx.scr2t(0), ctx.ps[0][:, 0:SLAB]))
                for fc in range(nfc):
                    p.g("vector", lambda e, fc=fc, s=s: e.tensor_mul(
                        gbuf[:, (fc * 4 + s) * SLAB:(fc * 4 + s + 1) * SLAB],
                        gbuf[:, (fc * 4 + s) * SLAB:(fc * 4 + s + 1) * SLAB], ctx.scr2t(0)))
        if ex < 16:
            load_w_bf16(ctx, w2e[ex * 512:(ex + 1) * 512, :], 512, 1024, wba)
        else:
            load_w_bf16(ctx, w2s[(ex - 16) * 512:(ex - 16) * 512 + 512, :], 512, 1024, wba)
        for s in range(4):
            for oc in range(8):
                p.mm(psP[:, 0:SLAB], [wview(wba, k, 1024)[:, oc * 128:(oc + 1) * 128] for k in range(nfc)],
                     [gbuf[:, (k * 4 + s) * SLAB:(k * 4 + s + 1) * SLAB] for k in range(nfc)])
                p.g("vector", lambda e, oc=oc, s=s: e.tensor_add(
                    acc[:, oc * TOWN + s * SLAB: oc * TOWN + (s + 1) * SLAB],
                    acc[:, oc * TOWN + s * SLAB: oc * TOWN + (s + 1) * SLAB], psP[:, 0:SLAB]))
    for s in range(4):
        p.dma(*[(ctx.scrt(i), x[i * 128:(i + 1) * 128, s * SLAB:(s + 1) * SLAB]) for i in range(8)])
        for oc in range(8):
            p.g("vector", lambda e, oc=oc, s=s: e.tensor_add(
                ctx.scr2t(oc), acc[:, oc * TOWN + s * SLAB: oc * TOWN + (s + 1) * SLAB], ctx.scrt(oc)))
        p.dma(*[(y[i * 128:(i + 1) * 128, s * SLAB:(s + 1) * SLAB], ctx.scr2t(i)) for i in range(8)])
    return p


def build_l0fa():
    KL = 2304
    p = Prog()
    x = p.inp("x", [D, 2305], F32)
    wqkv = p.inp("wqkv", [D, 3 * D], F32)
    wo = p.inp("wo", [D, D], F32)
    qsv = p.inp("qsv", [128, 1], F32)
    ksv = p.inp("ksv", [128, 1], F32)
    masks = p.inp("masks", [8 * 4 * 128, 256], BF16)
    qcos = p.inp("qcos", [128, TOWN], BF16)
    qsin = p.inp("qsin", [128, TOWN], BF16)
    kcos = p.inp("kcos", [128, KL], BF16)
    ksin = p.inp("ksin", [128, KL], BF16)
    y = p.outp("y", [D, TOWN], F32)
    ctx = Ctx(p, nps=2, scr_w=4104)
    psS = p.ps([128, 1024])
    psO = p.ps([64, 256])
    psP = p.ps([128, 512])
    maskb = p.sb([128, 32 * 256], BF16)
    p.dma(*[(maskb[:, i * 256:(i + 1) * 256], masks[i * 128:(i + 1) * 128, :]) for i in range(32)])
    tabq = p.sb([128, 2 * TOWN], BF16); tabk = p.sb([128, 2 * KL], BF16)
    p.dma((tabq[:, 0:TOWN], qcos[:, :]), (tabq[:, TOWN:], qsin[:, :]),
          (tabk[:, 0:KL], kcos[:, :]), (tabk[:, KL:], ksin[:, :]))
    qss = p.sb([128, 1]); kss = p.sb([128, 1])
    p.dma((qss[:, :], qsv[:, :]), (kss[:, :], ksv[:, :]))
    p.g("vector", lambda e: e.tensor_scalar(qss[:, :], qss[:, :], 8.0, None, OP.mult))
    kth = [p.sb([128, KL], BF16) for _ in range(2)]
    vtb = p.sb([128, 18 * 256], BF16)
    qth = p.sb([128, 2 * 256], BF16)
    otb = p.sb([128, 8 * TOWN], BF16)
    pbuf = p.sb([128, 4 * 256], BF16)
    xls = p.sb([128, 8 * 513], BF16)
    wb = p.sb([128, 8 * 256], BF16)
    wb2 = p.sb([128, 8 * 256], BF16)
    wob = p.sb([128, 8 * 1024], BF16)
    fbuf = p.sb([128, 512])
    otmp = p.sb([64, 256], BF16)
    widths = [512, 512, 512, 512, 256]
    def l2n_scale(dst_pair, src_ps, C, sc128, roped_dst2, ctab, stab):
        p.g("vector", lambda e: e.tensor_copy(fbuf[:, 0:C], src_ps))
        p.act(ctx.scr[:, 3 * C:4 * C][:, 0:C] if False else ctx.scr2[:, 3072:3072 + C], fbuf[:, 0:C], AF.Square)
        sqt = ctx.scr2[:, 3072:3072 + C]
        for h2 in range(2):
            b = h2 * 64
            p.mm(ctx.ps1[0:1, 0:C], [ctx.ones_col_f[b:b + 64, 0:1]], [sqt[b:b + 64, 0:C]])
            nr = ctx.rows[0:1, 0:C]
            ctx.p.act(nr, ctx.ps1[0:1, 0:C], AF.Sqrt)
            p.g("vector", lambda e: e.tensor_scalar(nr, nr, 1e-12, None, OP.max))
            rc = ctx.rows2[0:1, 0:C]
            p.g("vector", lambda e: e.reciprocal(rc, nr))
            p.mm(ctx.ps[0][0:64, 0:C], [ctx.ones_row_f[0:1, 0:64]], [rc])
            if b == 0:
                p.g("vector", lambda e: e.tensor_mul(fbuf[0:64, 0:C], fbuf[0:64, 0:C], ctx.ps[0][0:64, 0:C]))
            else:
                t0 = ctx.scr2[0:64, 2048:2048 + C]
                p.g("vector", lambda e: e.tensor_copy(t0, fbuf[64:128, 0:C]))
                p.g("vector", lambda e: e.tensor_mul(t0, t0, ctx.ps[0][0:64, 0:C]))
                p.g("vector", lambda e: e.tensor_copy(fbuf[64:128, 0:C], t0))
        p.g("vector", lambda e: e.tensor_scalar(fbuf[:, 0:C], fbuf[:, 0:C], sc128[:, 0:1], None, OP.mult))
        rope_block(ctx, fbuf[:, 0:C], ctx.scr2[:, 0:C] if False else fbuf[:, 0:C], ctab, stab, C, roped_dst2)
    fbuf2 = p.sb([128, 512])
    def l2n_rope(src_ps, C, sc128, roped_dst2, ctab, stab):
        p.g("vector", lambda e: e.tensor_copy(fbuf[:, 0:C], src_ps))
        sqt = ctx.scr2[:, 3072:3072 + C]
        p.act(sqt, fbuf[:, 0:C], AF.Square)
        for h2 in range(2):
            b = h2 * 64
            p.mm(ctx.ps1[0:1, 0:C], [ctx.ones_col_f[b:b + 64, 0:1]], [sqt[b:b + 64, 0:C]])
            nr = ctx.rows[0:1, 0:C]
            ctx.p.act(nr, ctx.ps1[0:1, 0:C], AF.Sqrt)
            p.g("vector", lambda e, nr=nr: e.tensor_scalar(nr, nr, 1e-12, None, OP.max))
            rc = ctx.rows2[0:1, 0:C]
            p.g("vector", lambda e, nr=nr, rc=rc: e.reciprocal(rc, nr))
            p.mm(ctx.ps[0][0:64, 0:C], [ctx.ones_row_f[0:1, 0:64]], [rc])
            if b == 0:
                p.g("vector", lambda e: e.tensor_mul(fbuf[0:64, 0:C], fbuf[0:64, 0:C], ctx.ps[0][0:64, 0:C]))
            else:
                t0 = ctx.scr2[0:64, 2048:2048 + C]
                p.g("vector", lambda e: e.tensor_copy(t0, fbuf[64:128, 0:C]))
                p.g("vector", lambda e: e.tensor_mul(t0, t0, ctx.ps[0][0:64, 0:C]))
                p.g("vector", lambda e: e.tensor_copy(fbuf[64:128, 0:C], t0))
        p.g("vector", lambda e, C=C: e.tensor_scalar(fbuf[:, 0:C], fbuf[:, 0:C], sc128[:, 0:1], None, OP.mult))
        rope_block(ctx, fbuf[:, 0:C], fbuf2[:, 0:C], ctab, stab, C, roped_dst2)
    for q4 in range(4):
        load_w_bf16(ctx, wqkv[:, 1024 + q4 * 256: 1024 + (q4 + 1) * 256], D, 256, wb)
        load_w_bf16(ctx, wqkv[:, 2048 + q4 * 256: 2048 + (q4 + 1) * 256], D, 256, wb2)
        col0 = 0
        for s, W in enumerate(widths):
            p.dma(*[(ctx.scr[:, i * 513:i * 513 + W + 1], x[i * 128:(i + 1) * 128, col0:col0 + W + 1]) for i in range(8)])
            p.g("vector", *[(lambda e, i=i, W=W: e.tensor_copy(xls[:, i * 513:i * 513 + W + 1], ctx.scr[:, i * 513:i * 513 + W + 1])) for i in range(8)])
            def rhs(c, sh):
                a = 1 - sh
                return xls[:, c * 513 + a: c * 513 + a + W]
            for dc in range(2):
                p.mm(psP[:, 0:W], [wview(wb, k, 256)[:, dc * 128:(dc + 1) * 128] for k in range(8)],
                     [rhs(k, 0 if k < 4 else 1) for k in range(8)])
                def dst2(h2, dc=dc, col0=col0, W=W):
                    return kth[dc][h2 * 64:h2 * 64 + 64, col0:col0 + W]
                l2n_rope(psP[:, 0:W], W, kss,
                         dst2, tabk[:, col0:col0 + W], tabk[:, KL + col0:KL + col0 + W])
            for tt in range(W // 128):
                p.mm(psP[:, 0:256], [rhs(k, 0 if k < 4 else 1)[:, tt * 128:(tt + 1) * 128] for k in range(8)],
                     [wview(wb2, k, 256) for k in range(8)])
                ti = col0 // 128 + tt
                p.g("vector", lambda e, ti=ti: e.tensor_copy(vtb[:, ti * 256:(ti + 1) * 256], psP[:, 0:256]))
            col0 += W
        load_w_bf16(ctx, wqkv[:, q4 * 256:(q4 + 1) * 256], D, 256, wb)
        for m in range(8):
            p.dma(*[(ctx.scr[:, i * 257:(i + 1) * 257], x[i * 128:(i + 1) * 128, 256 + m * 256:256 + (m + 1) * 256 + 1]) for i in range(8)])
            p.g("vector", *[(lambda e, i=i: e.tensor_copy(xls[:, i * 513:i * 513 + 257], ctx.scr[:, i * 257:(i + 1) * 257])) for i in range(8)])
            def qrhs(c, sh):
                a = 1 - sh
                return xls[:, c * 513 + a: c * 513 + a + 256]
            for dc in range(2):
                p.mm(psP[:, 0:256], [wview(wb, k, 256)[:, dc * 128:(dc + 1) * 128] for k in range(8)],
                     [qrhs(k, 0 if k < 4 else 1) for k in range(8)])
                l2n_rope(psP[:, 0:256], 256, qss,
                         lambda h2, dc=dc: qth[h2 * 64:h2 * 64 + 64, dc * 256:(dc + 1) * 256],
                         tabq[:, m * 256:(m + 1) * 256], tabq[:, TOWN + m * 256:TOWN + (m + 1) * 256])
            for hq in range(4):
                h = q4 * 4 + hq
                b = (hq % 2) * 64
                bcs = attn_unit(
                    ctx, 4,
                    lambda c, hq=hq, b=b, m=m: kth[hq // 2][b:b + 64, m * 256 + c * 128: m * 256 + (c + 1) * 128],
                    qth[b:b + 64, (hq // 2) * 256:(hq // 2) * 256 + 256],
                    lambda c, hq=hq, m=m: vtb[:, (m * 2 + c) * 256 + hq * 64: (m * 2 + c) * 256 + hq * 64 + 64],
                    psS, pbuf, psO, 256, 1.0,
                    mask_fn=lambda c, m=m: maskb[:, (m * 4 + c) * 256:(m * 4 + c + 1) * 256])
                p.g("vector", lambda e, bcs=bcs: e.tensor_mul(otmp[0:64, 0:256], psO[0:64, 0:256], bcs))
                ot = otb[(h % 2) * 64:(h % 2) * 64 + 64, (h // 2) * TOWN + m * 256:(h // 2) * TOWN + (m + 1) * 256]
                p.g("vector", lambda e, ot=ot: e.tensor_copy(ot, otmp[0:64, 0:256]))
    load_w_bf16(ctx, wo, D, 1024, wob)
    for s in range(4):
        p.dma(*[(ctx.scrt(i), x[i * 128:(i + 1) * 128, 257 + s * SLAB: 257 + (s + 1) * SLAB]) for i in range(8)])
        for oc in range(8):
            p.mm(psP[:, 0:SLAB], [wview(wob, k, 1024)[:, oc * 128:(oc + 1) * 128] for k in range(8)],
                 [otb[:, k * TOWN + s * SLAB: k * TOWN + (s + 1) * SLAB] for k in range(8)])
            p.g("vector", lambda e, oc=oc: e.tensor_add(ctx.scr2t(oc), psP[:, 0:SLAB], ctx.scrt(oc)))
        p.dma(*[(y[i * 128:(i + 1) * 128, s * SLAB:(s + 1) * SLAB], ctx.scr2t(i)) for i in range(8)])
    return p

def build_l0ff():
    p = Prog()
    x = p.inp("x", [D, TOWN + 1], F32)
    xo = p.inp("xo", [D, TOWN], F32)
    w1 = p.inp("w1", [D, 5504], F32)
    b1 = p.inp("b1", [128, 43], F32)
    w2 = p.inp("w2", [2752, D], F32)
    b2 = p.inp("b2", [128, 8], F32)
    lng = p.inp("lng", [128, 8], F32)
    lnb = p.inp("lnb", [128, 8], F32)
    y = p.outp("y", [D, TOWN], F32)
    ctx = Ctx(p, nps=2, scr_w=4352)
    psP = p.ps([128, 512])
    xnb = p.sb([128, 8 * SLAB * 4], BF16)
    gb = p.sb([128, 22 * TOWN], BF16)
    w1c = p.sb([128, 8 * 128], BF16)
    w2c = p.sb([128, 22 * 128], BF16)
    b1s = p.sb([128, 43]); b2s = p.sb([128, 8]); lgs = p.sb([128, 8]); lbs = p.sb([128, 8])
    p.dma((b1s[:, :], b1[:, :]), (b2s[:, :], b2[:, :]), (lgs[:, :], lng[:, :]), (lbs[:, :], lnb[:, :]))
    for s in range(4):
        prs = [(ctx.scr[:, i * 513:(i + 1) * 513], x[i * 128:(i + 1) * 128, s * SLAB:s * SLAB + 513]) for i in range(8)]
        p.dma(*prs)
        xts = []
        for c in range(8):
            a = 1 if c < 4 else 0
            xts.append(ctx.scr[:, c * 513 + a: c * 513 + a + SLAB])
        ln_cm(ctx, xts, SLAB, [xnb[:, (c * 4 + s) * SLAB:(c * 4 + s + 1) * SLAB] for c in range(8)],
              gpp=[lgs[:, c:c + 1] for c in range(8)], bpp=[lbs[:, c:c + 1] for c in range(8)])
    def xnt(c, s):
        return xnb[:, (c * 4 + s) * SLAB:(c * 4 + s + 1) * SLAB]
    def gt(k, s):
        return gb[:, (k * 4 + s) * SLAB:(k * 4 + s + 1) * SLAB]
    for hc in range(43):
        p.dma((ctx.scr[:, 0:1024].rearrange("p (t j) -> p t j", j=128),
               w1[:, hc * 128:(hc + 1) * 128].rearrange("(t p) j -> p t j", p=128)))
        p.g("vector", *[(lambda e, t=t: e.tensor_copy(w1c[:, t * 128:(t + 1) * 128], ctx.scr[:, t * 128:(t + 1) * 128])) for t in range(8)])
        for s in range(4):
            pp = psP[:, 0:SLAB]
            p.mm(pp, [w1c[:, t * 128:(t + 1) * 128] for t in range(8)], [xnt(t, s) for t in range(8)])
            p.g("vector", lambda e, hc=hc: e.tensor_scalar(ctx.scr2[0:64, 0:SLAB], pp[0:64, 0:SLAB], b1s[0:64, hc:hc + 1], None, OP.add))
            ctx.p.act(ctx.scr2[64:128, 0:SLAB], pp[64:128, 0:SLAB], AF.Gelu, bias=b1s[64:128, hc:hc + 1])
            p.g("vector", lambda e: e.tensor_copy(ctx.scr2[0:64, 512:512 + SLAB], ctx.scr2[64:128, 0:SLAB]))
            t_ = hc // 2
            bb = (hc % 2) * 64
            p.g("vector", lambda e: e.tensor_mul(ctx.scr2[0:64, 1024:1024 + SLAB], ctx.scr2[0:64, 0:SLAB], ctx.scr2[0:64, 512:512 + SLAB]))
            p.g("vector", lambda e, t_=t_, bb=bb, s=s: e.tensor_copy(gt(t_, s)[bb:bb + 64, 0:SLAB], ctx.scr2[0:64, 1024:1024 + SLAB]))
    for oc in range(8):
        p.dma(*[(ctx.scr[:, q * 1024:(q + 1) * 1024].rearrange("p (t j) -> p t j", j=128),
                 w2[q * 1024:(q + 1) * 1024, oc * 128:(oc + 1) * 128].rearrange("(t p) j -> p t j", p=128))
                for q in range(2)])
        p.dma((ctx.scr[:, 2048:2048 + 640].rearrange("p (t j) -> p t j", j=128),
               w2[2048:2688, oc * 128:(oc + 1) * 128].rearrange("(t p) j -> p t j", p=128)),
              (ctx.scr[0:64, 2688:2816], w2[2688:2752, oc * 128:(oc + 1) * 128]))
        p.g("vector", *[(lambda e, t=t: e.tensor_copy(w2c[:, t * 128:(t + 1) * 128], ctx.scr[:, t * 128:(t + 1) * 128])) for t in range(21)])
        p.g("vector", lambda e: e.tensor_copy(w2c[0:64, 21 * 128:21 * 128 + 128], ctx.scr[0:64, 21 * 128:21 * 128 + 128]))
        for s in range(4):
            pp = psP[:, 0:SLAB]
            ls = [w2c[:, t * 128:(t + 1) * 128] for t in range(21)] + [w2c[0:64, 21 * 128:21 * 128 + 128]]
            rs = [gt(t, s) for t in range(21)] + [gt(21, s)[0:64, 0:SLAB]]
            p.mm(pp, ls, rs)
            p.dma((ctx.scrt(0), x[oc * 128:(oc + 1) * 128, 1 + s * SLAB: 1 + (s + 1) * SLAB]))
            p.dma((ctx.scrt(1), xo[oc * 128:(oc + 1) * 128, s * SLAB:(s + 1) * SLAB]))
            p.g("vector", lambda e, oc=oc: e.tensor_scalar(ctx.scr2t(0), pp, b2s[:, oc:oc + 1], None, OP.add))
            p.g("vector", lambda e: e.tensor_add(ctx.scr2t(0), ctx.scr2t(0), ctx.scrt(0)))
            p.g("vector", lambda e: e.tensor_add(ctx.scr2t(0), ctx.scr2t(0), ctx.scrt(1)))
            p.dma((y[oc * 128:(oc + 1) * 128, s * SLAB:(s + 1) * SLAB], ctx.scr2t(0)))
    return p

_PROG_BUILDERS = {
    "ln_only": build_ln_only,
    "l0a": build_l0a,
    "l2f": build_l2f,
    "l3f": build_l3f,
}



def shard_own(xcm, c, halo_left=0):
    b, h = c // 2, c % 2
    lo = h * TOWN - halo_left
    if lo >= 0:
        return np.ascontiguousarray(xcm[b][:, lo:h * TOWN + TOWN])
    pad = np.zeros((D, -lo), np.float32)
    return np.ascontiguousarray(np.concatenate([pad, xcm[b][:, 0:h * TOWN + TOWN]], axis=1))

def asm_own(shards):
    return [np.ascontiguousarray(np.concatenate([shards[2 * b], shards[2 * b + 1]], axis=1))
            for b in range(B)]

def launch_ln(xcm):
    p = get_prog("ln_only")
    res = runp(p, [{"x": shard_own(xcm, c)} for c in range(8)])
    return asm_own([r["y"] for r in res.results])

def launch_l0a(xcm, Wv, Wo):
    p = get_prog("l0a")
    res = runp(p, [{"x": shard_own(xcm, c), "wv": Wv, "wo": Wo} for c in range(8)])
    return asm_own([r["y"] for r in res.results])

def launch_l2f(xcm, W_r, W_k, W_v, mu_r, mu_k):
    p = get_prog("l2f")
    wr = np.ascontiguousarray(W_r.T); wk = np.ascontiguousarray(W_k.T); wv = np.ascontiguousarray(W_v.T)
    mur = np.ascontiguousarray(mu_r.reshape(8, 128).T)
    muk = np.ascontiguousarray(mu_k.reshape(8, 128).T)
    res = runp(p, [{"x": shard_own(xcm, c, halo_left=1), "wr": wr, "wk": wk, "wv": wv,
                    "mur": mur, "muk": muk} for c in range(8)])
    return asm_own([r["y"] for r in res.results])

def launch_l3f(xcm, k0, k1, k2, s0, s1, s2):
    p = get_prog("l3f")
    k0u = np.ascontiguousarray(k0).view(np.uint8)
    k1u = np.ascontiguousarray(k1).view(np.uint8)
    k2u = np.ascontiguousarray(k2).view(np.uint8)
    sv = [np.ascontiguousarray(np.asarray(s, np.float32).reshape(1, 1)) for s in (s0, s1, s2)]
    res = runp(p, [{"x": shard_own(xcm, c), "k0": k0u, "k1": k1u, "k2": k2u,
                    "sc0": sv[0], "sc1": sv[1], "sc2": sv[2]} for c in range(8)])
    return asm_own([r["y"] for r in res.results])


def rope_block(ctx, psrc, fbuf, cosT, sinT, C, dst2, qknorm_eps=None, scrbase=0):
    p = ctx.p
    p.g("vector", lambda e: e.tensor_copy(fbuf, psrc))
    fsw = ctx.scr[:, scrbase:scrbase + C]
    p1 = ctx.scr[:, scrbase + C:scrbase + 2 * C]
    p2 = ctx.scr[:, scrbase + 2 * C:scrbase + 3 * C]
    p.g("vector", *[(lambda e, b=b: e.tensor_copy(fsw[b:b + 32, 0:C], fbuf[b + 32:b + 64, 0:C])) for b in (0, 64)],
                 *[(lambda e, b=b: e.tensor_copy(fsw[b + 32:b + 64, 0:C], fbuf[b:b + 32, 0:C])) for b in (0, 64)])
    p.g("vector", lambda e: e.tensor_mul(p1[:, 0:C], fbuf[:, 0:C], cosT))
    p.g("vector", lambda e: e.tensor_mul(p2[:, 0:C], fsw[:, 0:C], sinT))
    if qknorm_eps is None:
        p.g("vector", *[(lambda e, h2=h2: e.tensor_add(dst2(h2), p1[h2 * 64:h2 * 64 + 64, 0:C], p2[h2 * 64:h2 * 64 + 64, 0:C])) for h2 in range(2)])
    else:
        p.g("vector", lambda e: e.tensor_add(fbuf[:, 0:C], p1[:, 0:C], p2[:, 0:C]))
        p.act(p1[:, 0:C], fbuf[:, 0:C], AF.Square)
        for h2 in range(2):
            b = h2 * 64
            p.mm(ctx.ps1[0:1, 0:C], [ctx.ones_col_f[b:b + 64, 0:1]], [p1[b:b + 64, 0:C]])
            ms = ctx.rows[0:1, 0:C]
            p.g("vector", lambda e, C=C: e.tensor_scalar(ms, ctx.ps1[0:1, 0:C], 1.0 / 64.0, None, OP.mult))
            sd = ctx.rows[0:1, 512:512 + C]
            ctx.p.act(sd, ms, AF.Sqrt, bias=qknorm_eps)
            rc = ctx.rows2[0:1, 0:C]
            p.g("vector", lambda e: e.reciprocal(rc, sd))
            p.mm(ctx.ps[0][0:64, 0:C], [ctx.ones_row_f[0:1, 0:64]], [rc])
            if b == 0:
                p.g("vector", lambda e: e.tensor_mul(dst2(0), fbuf[0:64, 0:C], ctx.ps[0][0:64, 0:C]))
            else:
                t0 = ctx.scr2[0:64, 2048:2048 + C]
                p.g("vector", lambda e, b=b: e.tensor_copy(t0, fbuf[b:b + 64, 0:C]))
                p.g("vector", lambda e: e.tensor_mul(t0, t0, ctx.ps[0][0:64, 0:C]))
                p.g("vector", lambda e, h2=h2: e.tensor_copy(dst2(h2), t0))

def attn_unit(ctx, nch, lhsT_fn, qt_rhs, vt_fn, psS, pbuf, psO, nq, scale, mask_fn=None, pk=128):
    p = ctx.p
    nbat = psS.shape[1] // nq if hasattr(psS, "shape") else 4
    nbat = min(4, max(1, nbat))
    for c0 in range(0, nch, nbat):
        cc = min(nbat, nch - c0)
        p.g("tensor", *[(lambda e, c=c0 + i, i=i: e.matmul(
            psS[:, i * nq:i * nq + nq], lhsT_fn(c), qt_rhs, start=True, stop=True)) for i in range(cc)])
        b0 = p.ctile(-SHIFT); ssc = p.ctile(scale)
        p.g("scalar", *[(lambda e, c=c0 + i, i=i: e.activation(
            pbuf[:, c * nq:c * nq + nq], psS[:, i * nq:i * nq + nq], AF.Exp, bias=b0, scale=ssc)) for i in range(cc)])
        if mask_fn is not None:
            fns = []
            for i in range(cc):
                m = mask_fn(c0 + i)
                if m is not None:
                    fns.append(lambda e, c=c0 + i, m=m: e.tensor_mul(pbuf[:, c * nq:c * nq + nq], pbuf[:, c * nq:c * nq + nq], m))
            if fns:
                p.g("vector", *fns)
    p.mm(ctx.ps1[0:1, 0:nq], [ctx.ones_col_b[:, 0:1]] * nch, [pbuf[:, c * nq:c * nq + nq] for c in range(nch)])
    p.mm(psO[0:64, 0:nq], [vt_fn(c) for c in range(nch)], [pbuf[:, c * nq:c * nq + nq] for c in range(nch)])
    rc = ctx.rows2[0:1, 0:nq]
    p.g("vector", lambda e: e.reciprocal(rc, ctx.ps1[0:1, 0:nq]))
    p.mm(ctx.ps[1][0:64, 0:nq], [ctx.ones_row_f[0:1, 0:64]], [rc])
    bcs = ctx.scr2[0:64, 0:nq]
    p.g("vector", lambda e: e.tensor_copy(bcs, ctx.ps[1][0:64, 0:nq]))
    return bcs

def build_l3a():
    p = Prog()
    x = p.inp("x", [D, N], F32)
    xq = p.inp("xq", [D, TOWN], F32)
    wq = p.inp("wq", [D, D], F32)
    wk = p.inp("wk", [D, D], F32)
    wv = p.inp("wv", [D, D], F32)
    wo = p.inp("wo", [D, D], F32)
    masks = p.inp("masks", [8 * 4 * 128, 256], BF16)
    qcos = p.inp("qcos", [128, TOWN], BF16)
    qsin = p.inp("qsin", [128, TOWN], BF16)
    kcos = p.inp("kcos", [128, N], BF16)
    ksin = p.inp("ksin", [128, N], BF16)
    y = p.outp("y", [D, TOWN], F32)
    ctx = Ctx(p, nps=2, scr_w=4096)
    psS = p.ps([128, 1024])
    psO = p.ps([64, 256])
    psP = p.ps([128, 512])
    maskb = p.sb([128, 32 * 256], BF16)
    p.dma(*[(maskb[:, i * 256:(i + 1) * 256], masks[i * 128:(i + 1) * 128, :]) for i in range(32)])
    tabq = p.sb([128, 2 * TOWN], BF16); tabk = p.sb([128, 2 * N], BF16)
    p.dma((tabq[:, 0:TOWN], qcos[:, :]), (tabq[:, TOWN:], qsin[:, :]),
          (tabk[:, 0:N], kcos[:, :]), (tabk[:, N:], ksin[:, :]))
    xnslab = p.sb([128, 8 * SLAB], BF16)
    qsrc = p.sb([128, 8 * 256], BF16)
    kth = [p.sb([128, N], BF16) for _ in range(2)]
    vtb = p.sb([128, 32 * 256], BF16)
    qtb = p.sb([128, 2 * 256], BF16)
    otb = p.sb([128, 8 * TOWN], BF16)
    pbuf = p.sb([128, 32 * 256], BF16)
    wqb = p.sb([128, 8 * 256], BF16)
    wkb = p.sb([128, 8 * 256], BF16)
    wvb = p.sb([128, 8 * 256], BF16)
    wob = p.sb([128, 8 * 1024], BF16)
    fbuf = p.sb([128, 512])
    otmp = p.sb([64, 256], BF16)
    for q4 in range(4):
        load_w_bf16(ctx, wq[:, q4 * 256:(q4 + 1) * 256], D, 256, wqb)
        load_w_bf16(ctx, wk[:, q4 * 256:(q4 + 1) * 256], D, 256, wkb)
        load_w_bf16(ctx, wv[:, q4 * 256:(q4 + 1) * 256], D, 256, wvb)
        for s in range(8):
            xts = [ctx.scrt(i) for i in range(8)]
            p.dma(*[(xts[i], x[i * 128:(i + 1) * 128, s * SLAB:(s + 1) * SLAB]) for i in range(8)])
            xns = [xnslab[:, i * SLAB:(i + 1) * SLAB] for i in range(8)]
            ln_cm(ctx, xts, SLAB, xns)
            for dc in range(2):
                p.mm(psP[:, 0:SLAB], [wview(wkb, k, 256)[:, dc * 128:(dc + 1) * 128] for k in range(8)], xns)
                rope_block(ctx, psP[:, 0:SLAB], fbuf[:, 0:SLAB],
                           tabk[:, s * SLAB:(s + 1) * SLAB], tabk[:, N + s * SLAB:N + (s + 1) * SLAB],
                           SLAB, lambda h2, dc=dc, s=s: kth[dc][h2 * 64:h2 * 64 + 64, s * SLAB:(s + 1) * SLAB])
            for tt in range(4):
                p.mm(psP[:, 0:256], [xns[k][:, tt * 128:(tt + 1) * 128] for k in range(8)],
                     [wview(wvb, k, 256) for k in range(8)])
                p.g("vector", lambda e, s=s, tt=tt: e.tensor_copy(vtb[:, (s * 4 + tt) * 256:(s * 4 + tt + 1) * 256], psP[:, 0:256]))
        for m in range(8):
            xts = [ctx.scrt(i, 256) for i in range(8)]
            p.dma(*[(xts[i], xq[i * 128:(i + 1) * 128, m * 256:(m + 1) * 256]) for i in range(8)])
            qs = [qsrc[:, i * 256:(i + 1) * 256] for i in range(8)]
            ln_cm(ctx, xts, 256, qs)
            for dc in range(2):
                p.mm(psP[:, 0:256], [wview(wqb, k, 256)[:, dc * 128:(dc + 1) * 128] for k in range(8)], qs)
                rope_block(ctx, psP[:, 0:256], fbuf[:, 0:256],
                           tabq[:, m * 256:(m + 1) * 256], tabq[:, TOWN + m * 256:TOWN + (m + 1) * 256],
                           256, lambda h2, dc=dc: qtb[h2 * 64:h2 * 64 + 64, dc * 256:(dc + 1) * 256])
            nch = 4 * m + 4
            for hq in range(4):
                h = q4 * 4 + hq
                b = (hq % 2) * 64
                bcs = attn_unit(
                    ctx, nch,
                    lambda c, hq=hq, b=b: kth[hq // 2][b:b + 64, c * 128:(c + 1) * 128],
                    qtb[b:b + 64, (hq // 2) * 256:(hq // 2) * 256 + 256],
                    lambda c, hq=hq: vtb[:, c * 256 + hq * 64: c * 256 + hq * 64 + 64],
                    psS, pbuf, psO, 256, 0.125,
                    mask_fn=lambda c, m=m, nch=nch: (maskb[:, (m * 4 + (c - (nch - 4))) * 256:(m * 4 + (c - (nch - 4)) + 1) * 256]
                                                     if c >= nch - 4 else None))
                ot = otb[(h % 2) * 64:(h % 2) * 64 + 64, (h // 2) * TOWN + m * 256:(h // 2) * TOWN + (m + 1) * 256]
                p.g("vector", lambda e, bcs=bcs: e.tensor_mul(otmp[0:64, 0:256], psO[0:64, 0:256], bcs))
                p.g("vector", lambda e, ot=ot: e.tensor_copy(ot, otmp[0:64, 0:256]))
    load_w_bf16(ctx, wo, D, 1024, wob)
    for s in range(4):
        p.dma(*[(ctx.scrt(i), xq[i * 128:(i + 1) * 128, s * SLAB:(s + 1) * SLAB]) for i in range(8)])
        for oc in range(8):
            p.mm(psP[:, 0:SLAB], [wview(wob, k, 1024)[:, oc * 128:(oc + 1) * 128] for k in range(8)],
                 [otb[:, k * TOWN + s * SLAB: k * TOWN + (s + 1) * SLAB] for k in range(8)])
            p.g("vector", lambda e, oc=oc: e.tensor_add(ctx.scr2t(oc), psP[:, 0:SLAB], ctx.scrt(oc)))
        p.dma(*[(y[i * 128:(i + 1) * 128, s * SLAB:(s + 1) * SLAB], ctx.scr2t(i)) for i in range(8)])
    return p

_PROG_BUILDERS = {
    "ln_only": build_ln_only,
    "l0a": build_l0a,
    "l2f": build_l2f,
    "l3f": build_l3f,
    "l3a": build_l3a,
    "l2ag": build_l2ag,
    "l2al": build_l2al,
    "l1a": build_l1a,
    "l1f": build_l1f,
    "l0fa": build_l0fa,
    "l0ff": build_l0ff,
}


def rope_perm():
    perm = np.arange(D).reshape(16, 64)
    perm = np.concatenate([perm[:, 0::2], perm[:, 1::2]], axis=1).reshape(-1)
    return perm

def rope_tabs(theta, T0):
    inv = 1.0 / theta ** (np.arange(0, 64, 2, dtype=np.float64) / 64)
    t = np.arange(T0, dtype=np.float64)
    ang = np.outer(inv, t)
    c = np.tile(np.cos(ang), (4, 1))
    s = np.tile(np.concatenate([-np.sin(ang), np.sin(ang)], axis=0), (2, 1))
    return c.astype(NBF), s.astype(NBF)

def causal_masks(par):
    mk = np.zeros((8, 4, 128, 256), np.float32)
    jq = np.arange(256)[None, :]
    pp = np.arange(128)[:, None]
    for m in range(8):
        g = 2 * m + par
        nch = 4 * m + 4
        for j in range(4):
            c = nch - 4 + j
            k0 = c * 128
            mk[m, j] = ((g * 256 + jq) >= (k0 + pp)).astype(np.float32)
    return mk.astype(NBF)

def shard_q(xcm, c):
    b, par = c // 2, c % 2
    cols = []
    for m in range(8):
        g = 2 * m + par
        cols.append(xcm[b][:, g * 256:(g + 1) * 256])
    return np.ascontiguousarray(np.concatenate(cols, axis=1))

def asm_q(shards):
    out = []
    for b in range(B):
        xb = np.empty((D, N), np.float32)
        for par in range(2):
            sh = shards[2 * b + par]
            for m in range(8):
                g = 2 * m + par
                xb[:, g * 256:(g + 1) * 256] = sh[:, m * 256:(m + 1) * 256]
        out.append(xb)
    return out

def launch_l3a(xcm, wq, wk, wv, wo):
    p = get_prog("l3a")
    perm = rope_perm()
    wq_l = np.ascontiguousarray(wq.T[:, perm])
    wk_l = np.ascontiguousarray(wk.T[:, perm])
    wv_l = np.ascontiguousarray(wv.T)
    wo_l = np.ascontiguousarray(wo.T)
    cosT, sinT = rope_tabs(10000.0, N)
    pcs = []
    for c in range(8):
        b, par = c // 2, c % 2
        qcols = np.concatenate([np.arange((2 * m + par) * 256, (2 * m + par + 1) * 256) for m in range(8)])
        pcs.append({"x": np.ascontiguousarray(xcm[b]), "xq": shard_q(xcm, c),
                    "wq": wq_l, "wk": wk_l, "wv": wv_l, "wo": wo_l,
                    "masks": np.ascontiguousarray(causal_masks(par).reshape(8 * 4 * 128, 256)),
                    "qcos": np.ascontiguousarray(cosT[:, qcols]), "qsin": np.ascontiguousarray(sinT[:, qcols]),
                    "kcos": cosT, "ksin": sinT})
    res = runp(p, pcs)
    return asm_q([r["y"] for r in res.results])


def perm_cols_heads(w, nheads=16, hd=64):
    cols = np.arange(w.shape[1]).reshape(nheads, hd)
    cols = np.concatenate([cols[:, 0::2], cols[:, 1::2]], axis=1).reshape(-1)
    return np.ascontiguousarray(w[:, cols])

def pp_layout(v):
    return np.ascontiguousarray(np.asarray(v, np.float32).reshape(8, 128).T)

def launch_l2ag(xcm, g_wq, g_kv_down, g_k_up, g_v_up, g_wo, norm1_w):
    p = get_prog("l2ag")
    wq_l = perm_cols_heads(np.asarray(g_wq))
    kup_l = perm_cols_heads(np.asarray(g_k_up), nheads=4)
    cosT, sinT = rope_tabs(1e6, N)
    pcs = []
    for c in range(8):
        b, par = c // 2, c % 2
        qcols = np.concatenate([np.arange((2 * m + par) * 256, (2 * m + par + 1) * 256) for m in range(8)])
        pcs.append({"x": np.ascontiguousarray(xcm[b]), "xq": shard_q(xcm, c),
                    "wq": wq_l, "kvd": np.ascontiguousarray(g_kv_down),
                    "kup": kup_l, "vup": np.ascontiguousarray(g_v_up),
                    "wo": np.ascontiguousarray(g_wo), "n1w": pp_layout(norm1_w),
                    "masks": np.ascontiguousarray(causal_masks(par).reshape(8 * 4 * 128, 256)),
                    "qcos": np.ascontiguousarray(cosT[:, qcols]), "qsin": np.ascontiguousarray(sinT[:, qcols]),
                    "kcos": cosT, "ksin": sinT})
    res = runp(p, pcs)
    return asm_q([r["y"] for r in res.results])

def launch_l2al(acm, xcm, l_wq, l_wk, l_wv, l_wo, norm2_w):
    p = get_prog("l2al")
    wq_l = perm_cols_heads(np.asarray(l_wq))
    wk_l = perm_cols_heads(np.asarray(l_wk), nheads=4)
    KL = 2304
    inv = 1.0 / 1e4 ** (np.arange(0, 64, 2, dtype=np.float64) / 64)
    pcs = []
    for c in range(8):
        b, par = c // 2, c % 2
        kpos = np.maximum(np.arange(par * 2048 - 256, par * 2048 + 2048), 0).astype(np.float64)
        qpos = np.arange(par * 2048, par * 2048 + 2048).astype(np.float64)
        kang = np.outer(inv, kpos); qang = np.outer(inv, qpos)
        kc = np.tile(np.cos(kang), (4, 1)).astype(NBF)
        ks = np.tile(np.concatenate([-np.sin(kang), np.sin(kang)], axis=0), (2, 1)).astype(NBF)
        qc = np.tile(np.cos(qang), (4, 1)).astype(NBF)
        qs = np.tile(np.concatenate([-np.sin(qang), np.sin(qang)], axis=0), (2, 1)).astype(NBF)
        mk = np.zeros((8, 4, 128, 256), np.float32)
        jq = np.arange(256)[None, :]; pk = np.arange(128)[:, None]
        for m in range(8):
            for cc in range(4):
                kreal = par * 2048 - 256 + m * 256 + cc * 128 + pk
                qreal = par * 2048 + m * 256 + jq
                dd = qreal - kreal
                mk[m, cc] = ((dd >= 0) & (dd < 256) & (kreal >= 0)).astype(np.float32)
        pcs.append({"x": shard_own(acm, c, halo_left=256), "xo": shard_own(xcm, c),
                    "wq": wq_l, "wk": wk_l, "wv": np.ascontiguousarray(l_wv),
                    "wo": np.ascontiguousarray(l_wo), "n2w": pp_layout(norm2_w),
                    "masks": np.ascontiguousarray(mk.astype(NBF).reshape(8 * 4 * 128, 256)),
                    "qcos": np.ascontiguousarray(qc), "qsin": np.ascontiguousarray(qs),
                    "kcos": np.ascontiguousarray(kc), "ksin": np.ascontiguousarray(ks)})
    res = runp(p, pcs)
    return asm_own([r["y"] for r in res.results])


def launch_l1a(xcm, w_qkv, w_proj, b_proj, lq1, lk1, lq2, lk2, sub_norm_w):
    p = get_prog("l1a")
    wt = np.ascontiguousarray(np.asarray(w_qkv).T)
    perm = []
    for pair in range(8):
        base = pair * 128
        perm += (list(range(base, base + 32)) + list(range(base + 64, base + 96)) +
                 list(range(base + 32, base + 64)) + list(range(base + 96, base + 128)))
    perm = np.asarray(perm)
    wt2 = wt.copy()
    wt2[:, 0:1024] = wt[:, 0:1024][:, perm]
    wt2[:, 1024:2048] = wt[:, 1024:2048][:, perm]
    wt = np.ascontiguousarray(wt2)
    pcs = []
    for c in range(8):
        b, h = c // 2, c % 2
        pcs.append({"x": np.ascontiguousarray(xcm[b]), "xq": shard_own(xcm, c),
                    "wqkv": wt, "wproj": np.ascontiguousarray(w_proj.T),
                    "bproj": pp_layout(b_proj),
                    "lq1": np.ascontiguousarray(lq1.reshape(1, 32)), "lk1": np.ascontiguousarray(lk1.reshape(1, 32)),
                    "lq2": np.ascontiguousarray(lq2.reshape(1, 32)), "lk2": np.ascontiguousarray(lk2.reshape(1, 32)),
                    "subw": np.ascontiguousarray(sub_norm_w.reshape(64, 1))})
    res = runp(p, pcs)
    return asm_own([r["y"] for r in res.results])

def launch_l1f(xcm, w_gate, w1_e, w3_e, w2_e, w1_s, w3_s, w2_s):
    p = get_prog("l1f")
    w1e = np.ascontiguousarray(np.asarray(w1_e).transpose(0, 2, 1).reshape(16 * D, 512))
    w3e = np.ascontiguousarray(np.asarray(w3_e).transpose(0, 2, 1).reshape(16 * D, 512))
    w2e = np.ascontiguousarray(np.asarray(w2_e).transpose(0, 2, 1).reshape(16 * 512, D))
    pcs = []
    for c in range(8):
        pcs.append({"x": shard_own(xcm, c), "wg": np.ascontiguousarray(w_gate.T),
                    "w1e": w1e, "w3e": w3e, "w2e": w2e,
                    "w1s": np.ascontiguousarray(w1_s.T), "w3s": np.ascontiguousarray(w3_s.T),
                    "w2s": np.ascontiguousarray(w2_s.T)})
    res = runp(p, pcs)
    return asm_own([r["y"] for r in res.results])


def xpos_tabs(pos, n=4096):
    half = np.arange(0, 64, 2, dtype=np.float64)
    inv = 1.0 / 10000.0 ** (half / 64.0)
    base_scale = (half + 0.4 * 64) / (1.4 * 64)
    pos = np.asarray(pos, np.float64)
    ang = np.outer(inv, pos)
    power = (pos - 2048.0) / 128.0
    sc = base_scale[:, None] ** power[None, :]
    c, s = np.cos(ang), np.sin(ang)
    def pack(cv, sv):
        c64 = np.concatenate([cv, cv], axis=0)
        s64 = np.concatenate([-sv, sv], axis=0)
        return (np.tile(c64, (2, 1)).astype(NBF), np.tile(s64, (2, 1)).astype(NBF))
    qc, qs = pack(c * sc, s * sc)
    kc, ks = pack(c / sc, s / sc)
    return qc, qs, kc, ks

def launch_l0fa(ycm, Wqkv_i, Wo_i, q_scale_i, k_scale_i):
    p = get_prog("l0fa")
    pcs = []
    for c in range(8):
        b, par = c // 2, c % 2
        own0 = par * 2048
        kpos = np.maximum(np.arange(own0 - 256, own0 + 2048), 0)
        qpos = np.arange(own0, own0 + 2048)
        qc, qs, _, _ = xpos_tabs(qpos)
        _, _, kc, ks = xpos_tabs(kpos)
        mk = np.zeros((8, 4, 128, 256), np.float32)
        jq = np.arange(256)[None, :]; pk = np.arange(128)[:, None]
        for m in range(8):
            for cc in range(4):
                kreal = own0 - 256 + m * 256 + cc * 128 + pk
                qreal = own0 + m * 256 + jq
                dd = qreal - kreal
                mk[m, cc] = ((dd >= 0) & (dd <= 256) & (kreal >= 0)).astype(np.float32)
        pcs.append({"x": shard_own(ycm, c, halo_left=257),
                    "wqkv": np.ascontiguousarray(Wqkv_i), "wo": np.ascontiguousarray(Wo_i),
                    "qsv": np.ascontiguousarray(np.tile(np.asarray(q_scale_i, np.float32), 2).reshape(128, 1)),
                    "ksv": np.ascontiguousarray(np.tile(np.asarray(k_scale_i, np.float32), 2).reshape(128, 1)),
                    "masks": np.ascontiguousarray(mk.astype(NBF).reshape(8 * 4 * 128, 256)),
                    "qcos": np.ascontiguousarray(qc), "qsin": np.ascontiguousarray(qs),
                    "kcos": np.ascontiguousarray(kc), "ksin": np.ascontiguousarray(ks)})
    res = runp(p, pcs)
    return asm_own([r["y"] for r in res.results])

def prep_l0ff_weights(W1_i, b1_i, W2_i):
    W1p = np.zeros((D, 5504), np.float32)
    b1p = np.zeros((128, 43), np.float32)
    a = np.asarray(W1_i[:, :2730]); g = np.asarray(W1_i[:, 2730:])
    ba = np.asarray(b1_i[:2730]); bg = np.asarray(b1_i[2730:])
    ap = np.zeros((D, 2752), np.float32); ap[:, :2730] = a
    gp = np.zeros((D, 2752), np.float32); gp[:, :2730] = g
    bap = np.zeros(2752, np.float32); bap[:2730] = ba
    bgp = np.zeros(2752, np.float32); bgp[:2730] = bg
    for c in range(43):
        W1p[:, c * 128:c * 128 + 64] = ap[:, c * 64:(c + 1) * 64]
        W1p[:, c * 128 + 64:(c + 1) * 128] = gp[:, c * 64:(c + 1) * 64]
        b1p[0:64, c] = bap[c * 64:(c + 1) * 64]
        b1p[64:128, c] = bgp[c * 64:(c + 1) * 64]
    W2p = np.zeros((2752, D), np.float32)
    W2p[:2730] = np.asarray(W2_i)
    return np.ascontiguousarray(W1p), np.ascontiguousarray(b1p), np.ascontiguousarray(W2p)

def launch_l0ff(ycm, xocm, W1p, b1p, W2p, b2_i, lng_i, lnb_i):
    p = get_prog("l0ff")
    pcs = []
    for c in range(8):
        pcs.append({"x": shard_own(ycm, c, halo_left=1),
                    "xo": (shard_own(xocm, c) if xocm is not None else np.zeros((D, TOWN), np.float32)),
                    "w1": W1p, "b1": b1p, "w2": W2p,
                    "b2": pp_layout(b2_i), "lng": pp_layout(lng_i), "lnb": pp_layout(lnb_i)})
    res = runp(p, pcs)
    return asm_own([r["y"] for r in res.results])

def launch_l0f(xcm, ins):
    ycm = launch_ln(xcm)
    for i in range(6):
        ycm = launch_l0fa(ycm, np.asarray(ins["L0_ffn_Wqkv"][i]), np.asarray(ins["L0_ffn_Wo"][i]),
                          ins["L0_ffn_q_scale"][i], ins["L0_ffn_k_scale"][i])
        W1p, b1p, W2p = prep_l0ff_weights(ins["L0_ffn_W1"][i], ins["L0_ffn_b1"][i], ins["L0_ffn_W2"][i])
        ycm = launch_l0ff(ycm, xcm if i == 5 else None, W1p, b1p, W2p,
                          ins["L0_ffn_b2"][i], ins["L0_ffn_ln_g"][i], ins["L0_ffn_ln_b"][i])
    return ycm

def kernel(**ins):
    ins = {k: np.asarray(v) for k, v in ins.items()}
    x = ins["L0_attn_x"]
    xcm = [np.ascontiguousarray(x[b].T.astype(np.float32)) for b in range(B)]
    xcm = launch_l0a(xcm, ins["L0_attn_Wv"], ins["L0_attn_Wo"])
    xcm = launch_l0f(xcm, ins)
    xcm = launch_l1a(xcm, ins["L1_attn_w_qkv"], ins["L1_attn_w_proj"], ins["L1_attn_b_proj"],
                     ins["L1_attn_lambda_q1"], ins["L1_attn_lambda_k1"], ins["L1_attn_lambda_q2"],
                     ins["L1_attn_lambda_k2"], ins["L1_attn_sub_norm_w"])
    xcm = launch_l1f(xcm, ins["L1_ffn_w_gate"], ins["L1_ffn_w1_e"], ins["L1_ffn_w3_e"],
                     ins["L1_ffn_w2_e"], ins["L1_ffn_w1_s"], ins["L1_ffn_w3_s"], ins["L1_ffn_w2_s"])
    acm = launch_l2ag(xcm, ins["L2_attn_g_wq"], ins["L2_attn_g_kv_down"], ins["L2_attn_g_k_up"],
                      ins["L2_attn_g_v_up"], ins["L2_attn_g_wo"], ins["L2_attn_norm1_w"])
    xcm = launch_l2al(acm, xcm, ins["L2_attn_l_wq"], ins["L2_attn_l_wk"], ins["L2_attn_l_wv"],
                      ins["L2_attn_l_wo"], ins["L2_attn_norm2_w"])
    xcm = launch_l2f(xcm, ins["L2_ffn_W_r"], ins["L2_ffn_W_k"], ins["L2_ffn_W_v"],
                     ins["L2_ffn_mu_r"], ins["L2_ffn_mu_k"])
    xcm = launch_l3a(xcm, ins["L3_attn_wq"], ins["L3_attn_wk"], ins["L3_attn_wv"], ins["L3_attn_wo"])
    xcm = launch_l3f(xcm, ins["L3_ffn_k0"], ins["L3_ffn_k1"], ins["L3_ffn_k2"],
                     ins["L3_ffn_s0"], ins["L3_ffn_s1"], ins["L3_ffn_s2"])
    out = np.stack([xcm[b].T for b in range(B)]).astype(np.float32)
    return np.ascontiguousarray(out)
